# version 06
# baseline (speedup 1.0000x reference)
import sys
sys.path.insert(0, "/opt/trn_rl_repo")
from contextlib import ExitStack

import numpy as np

import concourse.bass as bass
import concourse.mybir as mybir
from concourse.bass_utils import run_bass_kernel_spmd
from concourse.replica_groups import maybe_share_collective_output_space

F32 = mybir.dt.float32
BF16 = mybir.dt.bfloat16
U8 = mybir.dt.uint8
I8 = mybir.dt.int8
I32 = mybir.dt.int32
AF = mybir.ActivationFunctionType
ALU = mybir.AluOpType
AX = mybir.AxisListType

T = 2048
D = 1024
H = 16
F = 4096
NTC = T // 128
NDC = D // 128
NFC = F // 128
CEXP = 60.0
LN_EPS = 1e-5
Q_EPS = 1e-8
BATCHES = [3, 3, 3, 3, 3, 1]


class Counts:
    def __init__(self):
        self.n = {}
        self.marks = {}

    def inc(self, key, amt=1):
        self.n[key] = self.n.get(key, 0) + amt
        return self.n[key]

    def get(self, key):
        return self.n.get(key, 0)

    def mark(self, name, key):
        self.marks[name] = self.n.get(key, 0)

    def at(self, name):
        return self.marks[name]

    def has(self, name):
        return name in self.marks


def build_nc():
    nc = bass.Bass("TRN2", target_bir_lowering=False, detect_race_conditions=False)

    x_in = nc.dram_tensor("x", [T, D], F32, kind="ExternalInput").ap()
    wqkv = nc.dram_tensor("wqkv", [D, 3 * D], F32, kind="ExternalInput").ap()
    bqkv = nc.dram_tensor("bqkv", [3 * D], F32, kind="ExternalInput").ap()
    w1t = nc.dram_tensor("w1t", [D, F], F32, kind="ExternalInput").ap()
    b1 = nc.dram_tensor("b1", [F], F32, kind="ExternalInput").ap()
    w2t = nc.dram_tensor("w2t", [F, D], F32, kind="ExternalInput").ap()
    b2 = nc.dram_tensor("b2", [D], F32, kind="ExternalInput").ap()
    out = nc.dram_tensor("out", [T, D], F32, kind="ExternalOutput").ap()

    ln2r_hbm = nc.dram_tensor("ln2r_hbm", [T, D], F32, kind="Internal").ap()
    w1i8_hbm = nc.dram_tensor("w1i8_hbm", [D, F], I8, kind="Internal").ap()
    w2i8_hbm = nc.dram_tensor("w2i8_hbm", [F, D], I8, kind="Internal").ap()

    rg = [list(range(8))]
    sp1 = maybe_share_collective_output_space("AllGather", rg)
    cc_in1 = nc.dram_tensor("cc_in1", [1], F32, kind="Internal")
    cc_out1 = nc.dram_tensor("cc_out1", [8], F32, kind="Internal", addr_space=sp1)
    cc_in2 = nc.dram_tensor("cc_in2", [1], F32, kind="Internal")
    cc_out2 = nc.dram_tensor("cc_out2", [8], F32, kind="Internal", addr_space=sp1)

    id128_t = nc.inline_tensor(np.eye(128, dtype=np.float32), "id128")
    id65_t = nc.inline_tensor(np.eye(65, dtype=np.float32), "id65")

    c = Counts()
    ctx = ExitStack()
    with ctx:
        def sb(name, shape, dtype, side="left", stack=None):
            return (stack or ctx).enter_context(
                nc.sbuf_tensor(name, shape, dtype, side=side))

        id128 = sb("id128s", [128, 128], F32)
        id65 = sb("id65s", [65, 65], F32)
        ones1f = sb("ones1f", [1, 128], F32)
        ones1b = sb("ones1b", [1, 128], BF16)
        cneg = sb("cneg", [128, 1], F32)
        bqk_sb = sb("bqk_sb", [128, 16], F32)
        bv_row = sb("bv_row", [1, D], BF16)
        sc = sb("sc", [128, 8], F32)
        mx1 = sb("mx1", [128, 32], F32)
        scale1 = sb("scale1", [128, 32], F32)
        b1_sb = sb("b1_sb", [128, 32], F32)
        b1q = sb("b1q", [128, 32], F32)
        bi32 = sb("bi32", [128, 32], I32)
        bif = sb("bif", [128, 32], F32)
        mx2 = sb("mx2", [128, 8], F32)
        scale2 = sb("scale2", [128, 8], F32)
        b2_sb = sb("b2_sb", [128, 8], F32)
        b2q = sb("b2q", [128, 8], F32)
        tiny = sb("tiny", [128, 4], F32)
        g_row = sb("g_row", [1, 8], F32)
        g_one = sb("g_one", [1, 4], F32)
        st32 = sb("st32", [32, 128], F32)
        bqk_nat = sb("bqk_nat", [16, 128], F32)
        b1_nat = sb("b1_nat", [32, 128], F32)
        b2_nat = sb("b2_nat", [8, 128], F32)
        runmax = sb("runmax", [128, 1], F32)
        runmax2 = sb("runmax2", [128, 1], F32)

        sem_y = ctx.enter_context(nc.semaphore("sem_y"))
        sem_g = ctx.enter_context(nc.semaphore("sem_g"))
        sem_t = ctx.enter_context(nc.semaphore("sem_t"))
        sem_s = ctx.enter_context(nc.semaphore("sem_s"))
        sem_v = ctx.enter_context(nc.semaphore("sem_v"))
        sem_b = ctx.enter_context(nc.semaphore("sem_b"))
        sem_c = ctx.enter_context(nc.semaphore("sem_c"))

        block = ctx.enter_context(nc.Block())

        class EP:

            def __init__(self, e):
                object.__setattr__(self, "_e", e)
                object.__setattr__(self, "_last", None)

            def __getattr__(self, name):
                attr = getattr(self._e, name)
                if callable(attr):
                    def call(*a, **k):
                        r = attr(*a, **k)
                        if hasattr(r, "then_inc"):
                            object.__setattr__(self, "_last", r)
                        return r
                    return call
                return attr

            def fin(self, sem, n=1):
                return self._last.then_inc(sem, n)

        def _wrap(dec):
            def w(f):
                dec(lambda e: f(EP(e)))
            return w

        _bsync = _wrap(block.sync)
        _bgps = _wrap(block.gpsimd)
        _btens = _wrap(block.tensor)
        _bscal = _wrap(block.scalar)
        _bvec = _wrap(block.vector)

        nbar = [0]

        def barrier():
            nbar[0] += 1
            tgt = 5 * nbar[0]
            for dec in (_bsync, _bgps, _btens, _bscal, _bvec):
                def _f(e, tgt=tgt):
                    e.drain().then_inc(sem_b, 1)
                    e.wait_ge(sem_b, tgt)
                dec(_f)

        def t_mark(name):
            def _f(e):
                e.fin(sem_t, 1)
            return _f

        def emit_ln(x_tile, out_stage, relu, stat_t, pref,
                    s_extra_wait=None, v_extra_wait=None):
            @_bvec
            def _(e):
                if v_extra_wait is not None:
                    e.wait_ge(sem_s, v_extra_wait)
                e.bn_stats(out=stat_t[:, 0:6], in_=x_tile[:, 0:512])
                e.bn_stats(out=stat_t[:, 6:12], in_=x_tile[:, 512:1024]).then_inc(sem_v, 1)
                c.inc("v")
                e.wait_ge(sem_v, c.get("v"))
                e.bn_aggr(out=stat_t[:, 12:14], in_=stat_t[:, 0:12]).then_inc(sem_v, 1)
                c.inc("v")
                e.wait_ge(sem_v, c.get("v"))
                e.tensor_scalar_add(out=stat_t[:, 13:14], in0=stat_t[:, 13:14], scalar1=LN_EPS)
                e.reciprocal(out=stat_t[:, 14:15], in_=stat_t[:, 13:14]).then_inc(sem_v, 1)
                c.inc("v")
                c.mark(pref + "_vstat", "v")

            @_bscal
            def _(e):
                e.wait_ge(sem_v, c.at(pref + "_vstat"))
                e.activation(out=stat_t[:, 14:15], in_=stat_t[:, 14:15], func=AF.Sqrt).then_inc(sem_s, 1)
                c.inc("s")
                c.mark(pref + "_sqrt", "s")

            @_bvec
            def _(e):
                e.wait_ge(sem_s, c.at(pref + "_sqrt"))
                e.scalar_tensor_tensor(
                    out=stat_t[:, 15:16], in0=stat_t[:, 12:13], scalar=-1.0,
                    in1=stat_t[:, 14:15], op0=ALU.mult, op1=ALU.mult).then_inc(sem_v, 1)
                c.inc("v")
                c.mark(pref + "_vbias", "v")

            @_bscal
            def _(e):
                e.wait_ge(sem_v, c.at(pref + "_vbias"))
                if s_extra_wait is not None:
                    e.wait_ge(s_extra_wait[0], s_extra_wait[1])
                e.activation(out=out_stage[:], in_=x_tile[:],
                             func=(AF.Relu if relu else AF.Identity),
                             bias=stat_t[:, 15:16], scale=stat_t[:, 14:15]).then_inc(sem_s, 1)
                c.inc("s")
                c.mark(pref + "_apply", "s")

        actx = ExitStack()
        X = [sb(f"X{i}", [128, D], F32, "left", actx) for i in range(NTC)]
        V_sb = [sb(f"V{i}", [128, H * 65], BF16, "left", actx) for i in range(NTC)]
        QK = [[sb(f"QK{s}_{q}", [128, T], BF16, "left", actx) for q in range(2)]
              for s in range(2)]
        PT = [sb(f"PT{i}", [128, 1536], BF16, "left", actx) for i in range(2)]
        attS = sb("attS", [65, 512], F32, "left", actx)
        stg = [sb(f"stg{i}", [128, 64], F32, "left", actx) for i in range(4)]
        piece = [sb(f"piece{i}", [128, D], F32, "left", actx) for i in range(2)]
        ln1T = [sb(f"ln1T{i}", [128, T], BF16, "right", actx) for i in range(NDC)]
        lnst0 = sb("lnst0", [128, D], F32, "right", actx)
        stat = sb("stat", [128, 16], F32, "right", actx)
        wvctx = ExitStack()
        wv_sb = [sb(f"wv{i}", [128, D], BF16, "right", wvctx) for i in range(NDC)]

        pctx = ExitStack()
        ps_ping = pctx.enter_context(nc.psum_tensor("ps_ping", [128, 1536], F32))
        ps_pong = pctx.enter_context(nc.psum_tensor("ps_pong", [128, 1536], F32))
        ps_att = pctx.enter_context(nc.psum_tensor("ps_att", [128, 512], F32))
        ps_misc = pctx.enter_context(nc.psum_tensor("ps_misc", [128, 512], F32))

        @_bsync
        def _(e):
            for i in range(NTC):
                e.dma_start(out=X[i][:], in_=x_in[i * 128:(i + 1) * 128, :]).then_inc(sem_y, 16)
            e.dma_start(out=id128[:], in_=id128_t.ap()).then_inc(sem_y, 16)
            e.dma_start(out=id65[:], in_=id65_t.ap()).then_inc(sem_y, 16)
            e.dma_start(out=bqk_nat[:], in_=bqkv[0:2048].rearrange("(j p) -> j p", p=128)).then_inc(sem_y, 16)
            e.dma_start(out=b1_nat[:], in_=b1.rearrange("(j p) -> j p", p=128)).then_inc(sem_y, 16)
            e.dma_start(out=b2_nat[:], in_=b2.rearrange("(j p) -> j p", p=128)).then_inc(sem_y, 16)
            c.inc("y", 16 * (NTC + 5))
            c.mark("p0_loads", "y")
            e.wait_ge(sem_y, c.at("p0_loads"))

        @_bgps
        def _(e):
            for dc in range(NDC):
                e.dma_start(out=wv_sb[dc][:], in_=wqkv[dc * 128:(dc + 1) * 128, 2048:3072]).then_inc(sem_g, 16)
            e.dma_start(out=bv_row[:], in_=bqkv[None, 2048:3072]).then_inc(sem_g, 16)
            c.inc("g", 16 * 9)
            c.mark("p0_casts", "g")
            e.wait_ge(sem_g, c.at("p0_casts"))

        @_bvec
        def _(e):
            e.memset(ones1f[:], 1.0)
            e.memset(ones1b[:], 1.0)
            e.memset(cneg[:], -CEXP)
            e.memset(runmax[:], 0.0)
            e.memset(runmax2[:], 0.0)
            for i in range(NTC):
                e.memset(V_sb[i][:].rearrange("p (h o) -> p h o", o=65)[:, :, 64:65], 1.0)
            c.inc("v")
            e.fin(sem_v, 1)

        barrier()

        @_btens
        def _(e):
            e.transpose(out=ps_misc[:, 0:16], in_=bqk_nat[:], identity=id128[0:16, 0:16])
            e.transpose(out=ps_misc[:, 16:48], in_=b1_nat[:], identity=id128[0:32, 0:32])
            e.transpose(out=ps_misc[:, 48:56], in_=b2_nat[:], identity=id128[0:8, 0:8])
            e.fin(sem_t, 1)
            c.inc("t")
            c.mark("bias_tr", "t")

        @_bvec
        def _(e):
            e.wait_ge(sem_t, c.at("bias_tr"))
            e.tensor_copy(out=bqk_sb[:], in_=ps_misc[:, 0:16])
            e.tensor_copy(out=b1_sb[:], in_=ps_misc[:, 16:48])
            e.tensor_copy(out=b2_sb[:], in_=ps_misc[:, 48:56])
            e.fin(sem_v, 1)
            c.inc("v")
            c.mark("bias_ok", "v")

        for tc in range(NTC):
            sw = (sem_t, c.at(f"ln1_{tc-1}_tr")) if tc >= 1 else None
            vw = c.at(f"ln1_{tc-1}_apply") if tc >= 1 else None
            emit_ln(X[tc], lnst0, False, stat, f"ln1_{tc}",
                    s_extra_wait=sw, v_extra_wait=vw)

            @_btens
            def _(e, tc=tc):
                e.wait_ge(sem_s, c.at(f"ln1_{tc}_apply"))
                if tc >= 2:
                    e.wait_ge(sem_v, c.at(f"ln1_{tc-2}_evac"))
                tgt = ps_ping if tc % 2 == 0 else ps_pong
                for dc in range(NDC):
                    e.transpose(out=tgt[:, dc * 128:(dc + 1) * 128],
                                in_=lnst0[:, dc * 128:(dc + 1) * 128], identity=id128[:])
                e.fin(sem_t, 1)
                c.inc("t")
                c.mark(f"ln1_{tc}_tr", "t")

            @_bvec
            def _(e, tc=tc):
                e.wait_ge(sem_t, c.at(f"ln1_{tc}_tr"))
                tgt = ps_ping if tc % 2 == 0 else ps_pong
                for dc in range(NDC):
                    e.tensor_copy(out=ln1T[dc][:, tc * 128:(tc + 1) * 128],
                                  in_=tgt[:, dc * 128:(dc + 1) * 128])
                e.fin(sem_v, 1)
                c.inc("v")
                c.mark(f"ln1_{tc}_evac", "v")

        barrier()

        for tc in range(NTC):
            for g in range(2):
                @_btens
                def _(e, tc=tc, g=g):
                    if c.has("vc_prev_evac"):
                        e.wait_ge(sem_v, c.at("vc_prev_evac"))
                    for dc in range(NDC):
                        e.matmul(out=ps_misc[:], lhsT=ln1T[dc][:, tc * 128:(tc + 1) * 128],
                                 rhs=wv_sb[dc][:, g * 512:(g + 1) * 512],
                                 start=(dc == 0), stop=False)
                    e.matmul(out=ps_misc[:], lhsT=ones1b[:], rhs=bv_row[:, g * 512:(g + 1) * 512],
                             start=False, stop=True)
                    e.fin(sem_t, 1)
                    c.inc("t")
                    c.mark("vc_mm", "t")

                @_bvec
                def _(e, tc=tc, g=g):
                    e.wait_ge(sem_t, c.at("vc_mm"))
                    src = ps_misc[:].rearrange("p (h o) -> p h o", o=64)
                    dst = V_sb[tc][:].rearrange("p (h o) -> p h o", o=65)[:, g * 8:(g + 1) * 8, 0:64]
                    e.tensor_copy(out=dst, in_=src).then_inc(sem_v, 1)
                    c.inc("v")
                    c.mark("vc_prev_evac", "v")

        barrier()
        wvctx.close()
        jitw = [sb(f"jitw{i}", [128, 2048], BF16, "right", actx) for i in range(2)]

        gb = [0]
        gu = [0]

        def emit_qk_jit(hp, dst_set):
            @_bgps
            def _(e):
                if hp >= 2:
                    e.wait_ge(sem_t, c.at(f"qk_{hp-2}_mmdone"))
                for qk in range(2):
                    fs = qk * 1024 + hp * 128
                    src = wqkv.rearrange("(dc p) f -> p dc f", p=128)[:, :, fs:fs + 128]
                    dst = jitw[hp % 2][:].rearrange("p (qk dc f) -> p qk dc f", qk=2, f=128)[:, qk]
                    e.dma_start(out=dst, in_=src).then_inc(sem_g, 16)
                c.inc("g", 32)
                c.mark(f"qk_{hp}_w", "g")

            @_btens
            def _(e):
                e.wait_ge(sem_g, c.at(f"qk_{hp}_w"))
                if hp >= 2:
                    e.wait_ge(sem_t, c.at(f"att_{hp-2}_done"))
                if c.has("last_recip"):
                    e.wait_ge(sem_v, c.at("last_recip"))
                for qk in range(2):
                    for tq in range(4):
                        for dc in range(NDC):
                            e.matmul(out=ps_misc[:],
                                     lhsT=jitw[hp % 2][:, qk * 1024 + dc * 128: qk * 1024 + (dc + 1) * 128],
                                     rhs=ln1T[dc][:, tq * 512:(tq + 1) * 512],
                                     start=(dc == 0), stop=(dc == NDC - 1))
                        e.fin(sem_t, 1)
                        c.inc("t")
                        c.mark(f"qk_{hp}_{qk}_{tq}_mm", "t")
                        e.wait_ge(sem_s, c.get("s") + 1)
                        c.inc("s")
                        c.mark(f"qk_{hp}_{qk}_{tq}_ev", "s")
                c.mark(f"qk_{hp}_mmdone", "t")

            @_bscal
            def _(e):
                for qk in range(2):
                    for tq in range(4):
                        e.wait_ge(sem_t, c.at(f"qk_{hp}_{qk}_{tq}_mm"))
                        e.activation(out=QK[dst_set][qk][:, tq * 512:(tq + 1) * 512],
                                     in_=ps_misc[:], func=AF.Identity,
                                     bias=bqk_sb[:, qk * 8 + hp: qk * 8 + hp + 1]).then_inc(sem_s, 1)

        def emit_attention_unit(h, qq, qset):
            hh = h % 2
            u = f"a{h}_{qq}"
            un = gu[0]
            gu[0] += 1
            nb = len(BATCHES)
            jb0 = [sum(BATCHES[:k]) for k in range(nb)]
            kb = [gb[0] + k for k in range(nb)]
            gb[0] += nb
            qsl = QK[qset][0][hh * 64: hh * 64 + 64, qq * 512:(qq + 1) * 512]

            @_btens
            def _(e):
                for k in range(nb):
                    n = kb[k]
                    buf = ps_ping if n % 2 == 0 else ps_pong
                    if n >= 2:
                        e.wait_ge(sem_s, c.at(f"exp_g{n-2}"))
                    for j in range(BATCHES[k]):
                        jb = jb0[k] + j
                        e.matmul(out=buf[:, j * 512:(j + 1) * 512],
                                 lhsT=QK[qset][1][hh * 64: hh * 64 + 64, jb * 128:(jb + 1) * 128],
                                 rhs=qsl, start=True, stop=True)
                    e.fin(sem_t, 1)
                    c.inc("t")
                    c.mark(f"qkt_g{n}", "t")
                    e.wait_ge(sem_s, c.get("s") + 1)
                    c.inc("s")
                    c.mark(f"exp_g{n}", "s")
                    if k == 0 and un >= 1:
                        e.wait_ge(sem_v, c.at(f"attevac_u{un-1}"))
                    for j in range(BATCHES[k]):
                        jb = jb0[k] + j
                        e.matmul(out=ps_att[0:65, :],
                                 lhsT=V_sb[jb][:, h * 65:(h + 1) * 65],
                                 rhs=PT[n % 2][:, j * 512:(j + 1) * 512],
                                 start=(k == 0 and j == 0), stop=(k == nb - 1 and j == BATCHES[k] - 1))
                    e.fin(sem_t, 1)
                    c.inc("t")
                    c.mark(f"pv_g{n}", "t")

            @_bscal
            def _(e):
                for k in range(nb):
                    n = kb[k]
                    buf = ps_ping if n % 2 == 0 else ps_pong
                    e.wait_ge(sem_t, c.at(f"qkt_g{n}"))
                    if n >= 2:
                        e.wait_ge(sem_t, c.at(f"pv_g{n-2}"))
                    w = BATCHES[k] * 512
                    e.activation(out=PT[n % 2][:, 0:w], in_=buf[:, 0:w],
                                 func=AF.Exp, bias=cneg[:]).then_inc(sem_s, 1)

            @_bvec
            def _(e):
                e.wait_ge(sem_t, c.at(f"pv_g{kb[-1]}"))
                e.tensor_copy(out=attS[:], in_=ps_att[0:65, :]).then_inc(sem_v, 1)
                c.inc("v")
                c.mark(f"attevac_u{un}", "v")

            @_btens
            def _(e):
                e.wait_ge(sem_v, c.at(f"attevac_u{un}"))
                if c.has("last_recip"):
                    e.wait_ge(sem_v, c.at("last_recip"))
                for b in range(4):
                    e.transpose(out=ps_misc[:, b * 65:(b + 1) * 65],
                                in_=attS[:, b * 128:(b + 1) * 128], identity=id65[:])
                e.fin(sem_t, 1)
                c.inc("t")
                c.mark(f"{u}_tr", "t")

            @_bvec
            def _(e):
                e.wait_ge(sem_t, c.at(f"{u}_tr"))
                if un >= 1:
                    e.wait_ge(sem_y, c.at(f"fold_u{un-1}"))
                for b in range(4):
                    e.reciprocal(out=tiny[:, b:b + 1], in_=ps_misc[:, b * 65 + 64: b * 65 + 65])
                e.fin(sem_v, 1)
                c.inc("v")
                e.wait_ge(sem_v, c.get("v"))
                for b in range(4):
                    e.tensor_scalar_mul(out=stg[b][:], in0=ps_misc[:, b * 65: b * 65 + 64],
                                        scalar1=tiny[:, b:b + 1])
                e.fin(sem_v, 1)
                c.inc("v")
                c.mark(f"recip_u{un}", "v")
                c.mark("last_recip", "v")

            @_bsync
            def _(e):
                e.wait_ge(sem_v, c.at(f"recip_u{un}"))
                if un >= 2:
                    e.wait_ge(sem_v, c.at(f"padd_u{un-2}"))
                pc = piece[un % 2]
                r0 = qq * 32
                for b in range(4):
                    e.dma_start(out=pc[r0 + b * 8: r0 + (b + 1) * 8, :].rearrange("g (ci i) -> g ci i", i=64),
                                in_=stg[b][:]).then_inc(sem_y, 16)
                c.inc("y", 64)
                c.mark(f"fold_u{un}", "y")

            @_bvec
            def _(e):
                e.wait_ge(sem_y, c.at(f"fold_u{un}"))
                pc = piece[un % 2]
                rows = slice(qq * 32, qq * 32 + 32)
                e.tensor_add(out=X[h][rows, :], in0=X[h][rows, :], in1=pc[rows, :]).then_inc(sem_v, 1)
                c.inc("v")
                c.mark(f"padd_u{un}", "v")

        emit_qk_jit(0, 0)
        for hp in range(8):
            if hp + 1 < 8:
                emit_qk_jit(hp + 1, (hp + 1) % 2)
            for hh in range(2):
                h = 2 * hp + hh
                for qq in range(4):
                    emit_attention_unit(h, qq, hp % 2)

            c.mark(f"att_{hp}_done", "t")

        barrier()
        pctx.close()

        p4ctx = ExitStack()
        ps_a = p4ctx.enter_context(nc.psum_tensor("ps_a", [128, 1024], F32))

        for tc in range(NTC):
            sw = (sem_y, c.at(f"spill_{tc-1}")) if tc >= 1 else None
            vw = c.at(f"ln2_{tc-1}_apply") if tc >= 1 else None
            emit_ln(X[tc], lnst0, True, stat, f"ln2_{tc}",
                    s_extra_wait=sw, v_extra_wait=vw)

            @_bvec
            def _(e, tc=tc):
                e.wait_ge(sem_s, c.at(f"ln2_{tc}_apply"))
                e.reduce_max(out=tiny[:, 1:2], in_=lnst0[:], axis=AX.X).then_inc(sem_v, 1)
                c.inc("v")
                e.wait_ge(sem_v, c.get("v"))
                e.tensor_max(out=runmax[:], in0=runmax[:], in1=tiny[:, 1:2]).then_inc(sem_v, 1)
                c.inc("v")
                c.mark(f"ln2_{tc}_max", "v")

            @_bsync
            def _(e, tc=tc):
                e.wait_ge(sem_v, c.at(f"ln2_{tc}_max"))
                e.dma_start(out=ln2r_hbm[tc * 128:(tc + 1) * 128, :], in_=lnst0[:]).then_inc(sem_y, 16)
                e.dma_start(out=out[tc * 128:(tc + 1) * 128, :], in_=X[tc][:]).then_inc(sem_y, 16)
                c.inc("y", 32)
                c.mark(f"spill_{tc}", "y")

        @_bsync
        def _(e):
            e.wait_ge(sem_y, c.at(f"spill_{NTC-1}"))

        @_btens
        def _(e):
            e.wait_ge(sem_v, c.at(f"ln2_{NTC-1}_max"))
            e.transpose(out=ps_a[0:1, 0:128], in_=runmax[:], identity=id128[:]).then_inc(sem_t, 1)
            c.inc("t")
            c.mark("rmx_tr", "t")

        @_bvec
        def _(e):
            e.wait_ge(sem_t, c.at("rmx_tr"))
            e.reduce_max(out=g_one[0:1, 0:1], in_=ps_a[0:1, 0:128], axis=AX.X).then_inc(sem_v, 1)
            c.inc("v")
            c.mark("rmx_fold", "v")

        @_bgps
        def _(e):
            e.wait_ge(sem_v, c.at("rmx_fold"))
            e.dma_start(out=cc_in1.ap(), in_=g_one[0:1, 0:1]).then_inc(sem_g, 16)
            c.inc("g", 16)
            e.wait_ge(sem_g, c.get("g"))
            e.collective_compute("AllGather", ALU.bypass, replica_groups=rg,
                                 ins=[cc_in1.ap()], outs=[cc_out1.ap()]).then_inc(sem_c, 1)
            c.inc("cc", 1)
            e.wait_ge(sem_c, c.get("cc"))
            e.dma_start(out=g_row[0:1, :], in_=cc_out1.ap()[None, :]).then_inc(sem_g, 16)
            c.inc("g", 16)
            c.mark("ar1_done", "g")

        @_bvec
        def _(e):
            e.wait_ge(sem_g, c.at("ar1_done"))
            e.reduce_max(out=g_one[0:1, 1:2], in_=g_row[0:1, :], axis=AX.X).then_inc(sem_v, 1)
            c.inc("v")
            e.wait_ge(sem_v, c.get("v"))
            e.tensor_scalar_max(out=g_one[0:1, 1:2], in0=g_one[0:1, 1:2], scalar1=Q_EPS)
            e.reciprocal(out=g_one[0:1, 2:3], in_=g_one[0:1, 1:2]).then_inc(sem_v, 1)
            c.inc("v")
            e.wait_ge(sem_v, c.get("v"))
            e.tensor_scalar_mul(out=g_one[0:1, 2:3], in0=g_one[0:1, 2:3], scalar1=255.0)
            e.tensor_scalar_mul(out=g_one[0:1, 3:4], in0=g_one[0:1, 1:2], scalar1=1.0 / (255.0 * 127.0))
            e.fin(sem_v, 1)
            c.inc("v")
            c.mark("s1_math", "v")

        @_btens
        def _(e):
            e.wait_ge(sem_v, c.at("s1_math"))
            e.matmul(out=ps_a[:, 0:2], lhsT=ones1f[:], rhs=g_one[0:1, 2:4], start=True, stop=True)
            e.fin(sem_t, 1)
            c.inc("t")
            c.mark("s1_bc", "t")

        @_bvec
        def _(e):
            e.wait_ge(sem_t, c.at("s1_bc"))
            e.tensor_copy(out=sc[:, 0:2], in_=ps_a[:, 0:2]).then_inc(sem_v, 1)
            c.inc("v")
            c.mark("s1_ready", "v")

        barrier()
        p4ctx.close()
        actx.close()

        bctx = ExitStack()
        h_sb = [sb(f"h{i}", [128, T], BF16, "left", bctx) for i in range(NFC)]
        wst = [sb(f"wst{i}", [128, 512], F32, "right", bctx) for i in range(2)]
        mxr = sb("mxr", [128, 512], F32, "right", bctx)
        mnr = sb("mnr", [128, 512], F32, "right", bctx)
        i8st = [sb(f"i8st{i}", [128, 512], I8, "right", bctx) for i in range(2)]
        qst = sb("qst", [128, D], F32, "right", bctx)
        u8st = sb("u8st", [128, D], U8, "right", bctx)

        pctxb = ExitStack()
        ps_f = [pctxb.enter_context(nc.psum_tensor(f"ps_f{i}", [128, 512], F32)) for i in range(5)]
        ps_tr = pctxb.enter_context(nc.psum_tensor("ps_tr", [128, 1024], F32))
        ps_bc = pctxb.enter_context(nc.psum_tensor("ps_bc", [128, 512], F32))

        w1pctx = ExitStack()
        mxrow = sb("mxrow", [1, F], F32, "right", w1pctx)

        for fr in range(8):
            for dc in range(NDC):
                @_bsync
                def _(e, fr=fr, dc=dc):
                    k = c.get("w1p1")
                    if k >= 2:
                        e.wait_ge(sem_v, c.at(f"w1p1_mx_{k-2}"))
                    e.dma_start(out=wst[k % 2][:],
                                in_=w1t[dc * 128:(dc + 1) * 128, fr * 512:(fr + 1) * 512]).then_inc(sem_y, 16)
                    c.inc("y", 16)
                    c.mark(f"w1p1_ld_{k}", "y")

                @_bvec
                def _(e, fr=fr, dc=dc):
                    k = c.get("w1p1")
                    e.wait_ge(sem_y, c.at(f"w1p1_ld_{k}"))
                    if dc == 0:
                        e.tensor_copy(out=mxr[:], in_=wst[k % 2][:])
                        e.tensor_copy(out=mnr[:], in_=wst[k % 2][:])
                    else:
                        e.tensor_tensor(out=mxr[:], in0=wst[k % 2][:], in1=mxr[:], op=ALU.max)
                        e.tensor_tensor(out=mnr[:], in0=wst[k % 2][:], in1=mnr[:], op=ALU.min)
                    e.fin(sem_v, 1)
                    c.inc("v")
                    c.mark(f"w1p1_mx_{k}", "v")
                    c.inc("w1p1")

            @_bvec
            def _(e, fr=fr):
                e.wait_ge(sem_v, c.at(f"w1p1_mx_{c.get('w1p1')-1}"))
                e.tensor_scalar_mul(out=mnr[:], in0=mnr[:], scalar1=-1.0)
                e.tensor_max(out=mxr[:], in0=mxr[:], in1=mnr[:]).then_inc(sem_v, 1)
                c.inc("v")
                c.mark(f"w1abs_{fr}", "v")

            @_btens
            def _(e, fr=fr):
                e.wait_ge(sem_v, c.at(f"w1abs_{fr}"))
                if fr >= 1:
                    e.wait_ge(sem_v, c.at(f"w1fold_{fr-1}"))
                for b in range(4):
                    e.transpose(out=ps_tr[:, b * 128:(b + 1) * 128],
                                in_=mxr[:, b * 128:(b + 1) * 128], identity=id128[:])
                e.fin(sem_t, 1)
                c.inc("t")
                c.mark(f"w1tr_{fr}", "t")

            @_bvec
            def _(e, fr=fr):
                e.wait_ge(sem_t, c.at(f"w1tr_{fr}"))
                for b in range(4):
                    fc = fr * 4 + b
                    e.reduce_max(out=mx1[:, fc:fc + 1], in_=ps_tr[:, b * 128:(b + 1) * 128], axis=AX.X)
                e.fin(sem_v, 1)
                c.inc("v")
                c.mark(f"w1fold_{fr}", "v")

        @_bvec
        def _(e):
            e.wait_ge(sem_v, c.at("w1fold_7"))
            e.tensor_scalar_max(out=mx1[:], in0=mx1[:], scalar1=Q_EPS).then_inc(sem_v, 1)
            c.inc("v")
            c.mark("mx1_eps", "v")

        @_btens
        def _(e):
            e.wait_ge(sem_v, c.at("mx1_eps"))
            e.transpose(out=ps_tr[0:32, 0:128], in_=mx1[:], identity=id128[:]).then_inc(sem_t, 1)
            c.inc("t")
            c.mark("mx1row_tr", "t")

        @_bvec
        def _(e):
            e.wait_ge(sem_t, c.at("mx1row_tr"))
            e.tensor_copy(out=st32[0:32, :], in_=ps_tr[0:32, 0:128]).then_inc(sem_v, 1)
            c.inc("v")
            c.mark("mx1row_st", "v")

        @_bsync
        def _(e):
            e.wait_ge(sem_v, c.at("mx1row_st"))
            e.dma_start(out=mxrow[0:1, :].rearrange("o (fc p) -> o fc p", p=128),
                        in_=st32[0:32, :]).then_inc(sem_y, 16)
            c.inc("y", 16)
            c.mark("mx1row_dma", "y")

        @_bvec
        def _(e):
            e.wait_ge(sem_y, c.at("mx1row_dma"))
            e.reciprocal(out=mxrow[:], in_=mxrow[:]).then_inc(sem_v, 1)
            c.inc("v")
            e.wait_ge(sem_v, c.get("v"))
            e.tensor_scalar_mul(out=mxrow[:], in0=mxrow[:], scalar1=127.0)
            e.tensor_scalar_mul(out=scale1[:], in0=mx1[:], scalar1=sc[:, 1:2])
            e.reciprocal(out=b1q[:], in_=scale1[:]).then_inc(sem_v, 1)
            c.inc("v")
            e.wait_ge(sem_v, c.get("v"))
            e.tensor_mul(out=b1q[:], in0=b1q[:], in1=b1_sb[:]).then_inc(sem_v, 1)
            c.inc("v")
            e.wait_ge(sem_v, c.get("v"))
            e.tensor_copy(out=bi32[:], in_=b1q[:]).then_inc(sem_v, 1)
            c.inc("v")
            c.mark("b1i32", "v")

        @_bgps
        def _(e):
            e.wait_ge(sem_v, c.at("b1i32"))
            e.dma_start(out=bif[:], in_=bi32[:]).then_inc(sem_g, 16)
            c.inc("g", 16)
            c.mark("b1if", "g")

        @_bvec
        def _(e):
            e.wait_ge(sem_g, c.at("b1if"))
            e.tensor_mul(out=b1q[:], in0=bif[:], in1=scale1[:]).then_inc(sem_v, 1)
            c.inc("v")
            c.mark("b1q_ok", "v")

        for fr in range(8):
            @_btens
            def _(e, fr=fr):
                e.wait_ge(sem_v, c.at("b1q_ok"))
                if fr >= 1:
                    e.wait_ge(sem_v, c.at(f"w1q_{fr-1}_done"))
                e.matmul(out=ps_bc[:], lhsT=ones1f[:], rhs=mxrow[0:1, fr * 512:(fr + 1) * 512],
                         start=True, stop=True)
                e.fin(sem_t, 1)
                c.inc("t")
                c.mark(f"w1bc_{fr}", "t")

            for dc in range(NDC):
                @_bsync
                def _(e, fr=fr, dc=dc):
                    k = c.get("w1p2")
                    if k >= 2:
                        e.wait_ge(sem_v, c.at(f"w1p2_q_{k-2}"))
                    e.dma_start(out=wst[k % 2][:],
                                in_=w1t[dc * 128:(dc + 1) * 128, fr * 512:(fr + 1) * 512]).then_inc(sem_y, 16)
                    c.inc("y", 16)
                    c.mark(f"w1p2_ld_{k}", "y")

                @_bvec
                def _(e, fr=fr, dc=dc):
                    k = c.get("w1p2")
                    e.wait_ge(sem_y, c.at(f"w1p2_ld_{k}"))
                    e.wait_ge(sem_t, c.at(f"w1bc_{fr}"))
                    if k >= 2:
                        e.wait_ge(sem_y, c.at(f"w1p2_st_{k-2}"))
                    e.tensor_tensor(out=i8st[k % 2][:], in0=wst[k % 2][:], in1=ps_bc[:],
                                    op=ALU.mult).then_inc(sem_v, 1)
                    c.inc("v")
                    c.mark(f"w1p2_q_{k}", "v")

                @_bsync
                def _(e, fr=fr, dc=dc):
                    k = c.get("w1p2")
                    e.wait_ge(sem_v, c.at(f"w1p2_q_{k}"))
                    e.dma_start(out=w1i8_hbm[dc * 128:(dc + 1) * 128, fr * 512:(fr + 1) * 512],
                                in_=i8st[k % 2][:]).then_inc(sem_y, 16)
                    c.inc("y", 16)
                    c.mark(f"w1p2_st_{k}", "y")
                    c.inc("w1p2")

            @_bvec
            def _(e, fr=fr):
                e.wait_ge(sem_y, c.at(f"w1p2_st_{c.get('w1p2')-1}"))
                e.fin(sem_v, 1)
                c.inc("v")
                c.mark(f"w1q_{fr}_done", "v")

        barrier()
        w1pctx.close()

        b1ctx = ExitStack()
        XiT = [sb(f"XiT{i}", [128, T], BF16, "right", b1ctx) for i in range(NDC)]
        w1jit = [sb(f"w1jit{i}", [128, 1024], BF16, "right", b1ctx) for i in range(2)]

        for tc in range(NTC):
            @_bsync
            def _(e, tc=tc):
                if tc >= 1:
                    e.wait_ge(sem_t, c.at(f"xq_{tc-1}_tr"))
                e.dma_start(out=qst[:], in_=ln2r_hbm[tc * 128:(tc + 1) * 128, :]).then_inc(sem_y, 16)
                c.inc("y", 16)
                c.mark(f"xq_{tc}_ld", "y")

            @_bscal
            def _(e, tc=tc):
                e.wait_ge(sem_y, c.at(f"xq_{tc}_ld"))
                if tc >= 1:
                    e.wait_ge(sem_g, c.at(f"xq_{tc-1}_cast"))
                e.activation(out=u8st[:], in_=qst[:], func=AF.Identity,
                             scale=sc[:, 0:1]).then_inc(sem_s, 1)
                c.inc("s")
                c.mark(f"xq_{tc}_u8", "s")

            @_bgps
            def _(e, tc=tc):
                e.wait_ge(sem_s, c.at(f"xq_{tc}_u8"))
                e.dma_start(out=qst[:], in_=u8st[:]).then_inc(sem_g, 16)
                c.inc("g", 16)
                c.mark(f"xq_{tc}_cast", "g")

            @_btens
            def _(e, tc=tc):
                e.wait_ge(sem_g, c.at(f"xq_{tc}_cast"))
                if tc >= 1:
                    e.wait_ge(sem_v, c.at(f"xq_{tc-1}_evac"))
                for dc in range(NDC):
                    e.transpose(out=ps_tr[:, (dc % 4) * 256: (dc % 4) * 256 + 128] if False
                                else ps_tr[:, (dc % 8) * 128:(dc % 8) * 128 + 128],
                                in_=qst[:, dc * 128:(dc + 1) * 128], identity=id128[:])
                e.fin(sem_t, 1)
                c.inc("t")
                c.mark(f"xq_{tc}_tr", "t")

            @_bvec
            def _(e, tc=tc):
                e.wait_ge(sem_t, c.at(f"xq_{tc}_tr"))
                for dc in range(NDC):
                    e.tensor_copy(out=XiT[dc][:, tc * 128:(tc + 1) * 128],
                                  in_=ps_tr[:, dc * 128:(dc + 1) * 128])
                e.fin(sem_v, 1)
                c.inc("v")
                c.mark(f"xq_{tc}_evac", "v")

        barrier()

        for fc in range(NFC):
            @_bgps
            def _(e, fc=fc):
                if fc >= 2:
                    e.wait_ge(sem_t, c.at(f"f1_{fc-2}_mm"))
                src = w1i8_hbm.rearrange("(dc p) f -> p dc f", p=128)[:, :, fc * 128:(fc + 1) * 128]
                dst = w1jit[fc % 2][:].rearrange("p (dc f) -> p dc f", f=128)
                e.dma_start(out=dst, in_=src).then_inc(sem_g, 16)
                c.inc("g", 16)
                c.mark(f"f1_{fc}_w", "g")

            @_btens
            def _(e, fc=fc):
                e.wait_ge(sem_g, c.at(f"f1_{fc}_w"))
                if fc >= 1:
                    e.wait_ge(sem_s, c.at(f"f1_{fc-1}_h"))
                for dc in range(NDC):
                    for tq in range(4):
                        e.matmul(out=ps_f[tq][:],
                                 lhsT=w1jit[fc % 2][:, dc * 128:(dc + 1) * 128],
                                 rhs=XiT[dc][:, tq * 512:(tq + 1) * 512],
                                 start=(dc == 0), stop=(dc == NDC - 1))
                e.fin(sem_t, 1)
                c.inc("t")
                c.mark(f"f1_{fc}_mm", "t")

            @_bscal
            def _(e, fc=fc):
                e.wait_ge(sem_t, c.at(f"f1_{fc}_mm"))
                for tq in range(4):
                    e.activation(out=h_sb[fc][:, tq * 512:(tq + 1) * 512], in_=ps_f[tq][:],
                                 func=AF.Relu, bias=b1q[:, fc:fc + 1], scale=scale1[:, fc:fc + 1])
                e.fin(sem_s, 1)
                c.inc("s")
                c.mark(f"f1_{fc}_h", "s")

            @_bvec
            def _(e, fc=fc):
                e.wait_ge(sem_s, c.at(f"f1_{fc}_h"))
                e.reduce_max(out=tiny[:, 2:3], in_=h_sb[fc][:], axis=AX.X).then_inc(sem_v, 1)
                c.inc("v")
                e.wait_ge(sem_v, c.get("v"))
                e.tensor_max(out=runmax2[:], in0=runmax2[:], in1=tiny[:, 2:3]).then_inc(sem_v, 1)
                c.inc("v")
                c.mark(f"f1_{fc}_max", "v")

        barrier()
        b1ctx.close()

        @_btens
        def _(e):
            e.transpose(out=ps_tr[0:1, 0:128], in_=runmax2[:], identity=id128[:]).then_inc(sem_t, 1)
            c.inc("t")
            c.mark("rmx2_tr", "t")

        @_bvec
        def _(e):
            e.wait_ge(sem_t, c.at("rmx2_tr"))
            e.reduce_max(out=g_one[0:1, 0:1], in_=ps_tr[0:1, 0:128], axis=AX.X).then_inc(sem_v, 1)
            c.inc("v")
            c.mark("rmx2_fold", "v")

        @_bgps
        def _(e):
            e.wait_ge(sem_v, c.at("rmx2_fold"))
            e.dma_start(out=cc_in2.ap(), in_=g_one[0:1, 0:1]).then_inc(sem_g, 16)
            c.inc("g", 16)
            e.wait_ge(sem_g, c.get("g"))
            e.collective_compute("AllGather", ALU.bypass, replica_groups=rg,
                                 ins=[cc_in2.ap()], outs=[cc_out2.ap()]).then_inc(sem_c, 1)
            c.inc("cc", 1)
            e.wait_ge(sem_c, c.get("cc"))
            e.dma_start(out=g_row[0:1, :], in_=cc_out2.ap()[None, :]).then_inc(sem_g, 16)
            c.inc("g", 16)
            c.mark("ar2_done", "g")

        @_bvec
        def _(e):
            e.wait_ge(sem_g, c.at("ar2_done"))
            e.reduce_max(out=g_one[0:1, 1:2], in_=g_row[0:1, :], axis=AX.X).then_inc(sem_v, 1)
            c.inc("v")
            e.wait_ge(sem_v, c.get("v"))
            e.tensor_scalar_max(out=g_one[0:1, 1:2], in0=g_one[0:1, 1:2], scalar1=Q_EPS)
            e.reciprocal(out=g_one[0:1, 2:3], in_=g_one[0:1, 1:2]).then_inc(sem_v, 1)
            c.inc("v")
            e.wait_ge(sem_v, c.get("v"))
            e.tensor_scalar_mul(out=g_one[0:1, 2:3], in0=g_one[0:1, 2:3], scalar1=255.0)
            e.tensor_scalar_mul(out=g_one[0:1, 3:4], in0=g_one[0:1, 1:2], scalar1=1.0 / (255.0 * 127.0))
            e.fin(sem_v, 1)
            c.inc("v")
            c.mark("s2_math", "v")

        @_btens
        def _(e):
            e.wait_ge(sem_v, c.at("s2_math"))
            e.matmul(out=ps_bc[:, 0:2], lhsT=ones1f[:], rhs=g_one[0:1, 2:4], start=True, stop=True)
            e.fin(sem_t, 1)
            c.inc("t")
            c.mark("s2_bc", "t")

        @_bvec
        def _(e):
            e.wait_ge(sem_t, c.at("s2_bc"))
            e.tensor_copy(out=sc[:, 2:4], in_=ps_bc[:, 0:2]).then_inc(sem_v, 1)
            c.inc("v")
            c.mark("s2_ready", "v")

        w2pctx = ExitStack()
        mxrow2 = sb("mxrow2", [1, D], F32, "right", w2pctx)

        for dr in range(2):
            for fc in range(NFC):
                @_bsync
                def _(e, dr=dr, fc=fc):
                    k = c.get("w2p1")
                    if k >= 2:
                        e.wait_ge(sem_v, c.at(f"w2p1_mx_{k-2}"))
                    e.dma_start(out=wst[k % 2][:],
                                in_=w2t[fc * 128:(fc + 1) * 128, dr * 512:(dr + 1) * 512]).then_inc(sem_y, 16)
                    c.inc("y", 16)
                    c.mark(f"w2p1_ld_{k}", "y")

                @_bvec
                def _(e, dr=dr, fc=fc):
                    k = c.get("w2p1")
                    e.wait_ge(sem_y, c.at(f"w2p1_ld_{k}"))
                    if fc == 0:
                        e.tensor_copy(out=mxr[:], in_=wst[k % 2][:])
                        e.tensor_copy(out=mnr[:], in_=wst[k % 2][:])
                    else:
                        e.tensor_tensor(out=mxr[:], in0=wst[k % 2][:], in1=mxr[:], op=ALU.max)
                        e.tensor_tensor(out=mnr[:], in0=wst[k % 2][:], in1=mnr[:], op=ALU.min)
                    e.fin(sem_v, 1)
                    c.inc("v")
                    c.mark(f"w2p1_mx_{k}", "v")
                    c.inc("w2p1")

            @_bvec
            def _(e, dr=dr):
                e.wait_ge(sem_v, c.at(f"w2p1_mx_{c.get('w2p1')-1}"))
                e.tensor_scalar_mul(out=mnr[:], in0=mnr[:], scalar1=-1.0)
                e.tensor_max(out=mxr[:], in0=mxr[:], in1=mnr[:]).then_inc(sem_v, 1)
                c.inc("v")
                c.mark(f"w2abs_{dr}", "v")

            @_btens
            def _(e, dr=dr):
                e.wait_ge(sem_v, c.at(f"w2abs_{dr}"))
                if dr >= 1:
                    e.wait_ge(sem_v, c.at(f"w2fold_{dr-1}"))
                for b in range(4):
                    e.transpose(out=ps_tr[:, b * 128:(b + 1) * 128],
                                in_=mxr[:, b * 128:(b + 1) * 128], identity=id128[:])
                e.fin(sem_t, 1)
                c.inc("t")
                c.mark(f"w2tr_{dr}", "t")

            @_bvec
            def _(e, dr=dr):
                e.wait_ge(sem_t, c.at(f"w2tr_{dr}"))
                for b in range(4):
                    dc = dr * 4 + b
                    e.reduce_max(out=mx2[:, dc:dc + 1], in_=ps_tr[:, b * 128:(b + 1) * 128], axis=AX.X)
                e.fin(sem_v, 1)
                c.inc("v")
                c.mark(f"w2fold_{dr}", "v")

        @_bvec
        def _(e):
            e.wait_ge(sem_v, c.at("w2fold_1"))
            e.tensor_scalar_max(out=mx2[:], in0=mx2[:], scalar1=Q_EPS).then_inc(sem_v, 1)
            c.inc("v")
            c.mark("mx2_eps", "v")

        @_btens
        def _(e):
            e.wait_ge(sem_v, c.at("mx2_eps"))
            e.transpose(out=ps_tr[0:8, 0:128], in_=mx2[:], identity=id128[:]).then_inc(sem_t, 1)
            c.inc("t")
            c.mark("mx2row_tr", "t")

        @_bvec
        def _(e):
            e.wait_ge(sem_t, c.at("mx2row_tr"))
            e.tensor_copy(out=st32[0:8, :], in_=ps_tr[0:8, 0:128]).then_inc(sem_v, 1)
            c.inc("v")
            c.mark("mx2row_st", "v")

        @_bsync
        def _(e):
            e.wait_ge(sem_v, c.at("mx2row_st"))
            e.dma_start(out=mxrow2[0:1, :].rearrange("o (dc p) -> o dc p", p=128),
                        in_=st32[0:8, :]).then_inc(sem_y, 16)
            c.inc("y", 16)
            c.mark("mx2row_dma", "y")

        @_bvec
        def _(e):
            e.wait_ge(sem_y, c.at("mx2row_dma"))
            e.wait_ge(sem_v, c.at("s2_ready"))
            e.reciprocal(out=mxrow2[:], in_=mxrow2[:]).then_inc(sem_v, 1)
            c.inc("v")
            e.wait_ge(sem_v, c.get("v"))
            e.tensor_scalar_mul(out=mxrow2[:], in0=mxrow2[:], scalar1=127.0)
            e.tensor_scalar_mul(out=scale2[:], in0=mx2[:], scalar1=sc[:, 3:4])
            e.reciprocal(out=b2q[:], in_=scale2[:]).then_inc(sem_v, 1)
            c.inc("v")
            e.wait_ge(sem_v, c.get("v"))
            e.tensor_mul(out=b2q[:], in0=b2q[:], in1=b2_sb[:]).then_inc(sem_v, 1)
            c.inc("v")
            e.wait_ge(sem_v, c.get("v"))
            e.tensor_copy(out=bi32[:, 0:8], in_=b2q[:]).then_inc(sem_v, 1)
            c.inc("v")
            c.mark("b2i32", "v")

        @_bgps
        def _(e):
            e.wait_ge(sem_v, c.at("b2i32"))
            e.dma_start(out=bif[:, 0:8], in_=bi32[:, 0:8]).then_inc(sem_g, 16)
            c.inc("g", 16)
            c.mark("b2if", "g")

        @_bvec
        def _(e):
            e.wait_ge(sem_g, c.at("b2if"))
            e.tensor_mul(out=b2q[:], in0=bif[:, 0:8], in1=scale2[:]).then_inc(sem_v, 1)
            c.inc("v")
            c.mark("b2q_ok", "v")

        for dr in range(2):
            @_btens
            def _(e, dr=dr):
                e.wait_ge(sem_v, c.at("b2q_ok"))
                if dr >= 1:
                    e.wait_ge(sem_v, c.at(f"w2q_{dr-1}_done"))
                e.matmul(out=ps_bc[:], lhsT=ones1f[:], rhs=mxrow2[0:1, dr * 512:(dr + 1) * 512],
                         start=True, stop=True)
                e.fin(sem_t, 1)
                c.inc("t")
                c.mark(f"w2bc_{dr}", "t")

            for fc in range(NFC):
                @_bsync
                def _(e, dr=dr, fc=fc):
                    k = c.get("w2p2")
                    if k >= 2:
                        e.wait_ge(sem_v, c.at(f"w2p2_q_{k-2}"))
                    e.dma_start(out=wst[k % 2][:],
                                in_=w2t[fc * 128:(fc + 1) * 128, dr * 512:(dr + 1) * 512]).then_inc(sem_y, 16)
                    c.inc("y", 16)
                    c.mark(f"w2p2_ld_{k}", "y")

                @_bvec
                def _(e, dr=dr, fc=fc):
                    k = c.get("w2p2")
                    e.wait_ge(sem_y, c.at(f"w2p2_ld_{k}"))
                    e.wait_ge(sem_t, c.at(f"w2bc_{dr}"))
                    if k >= 2:
                        e.wait_ge(sem_y, c.at(f"w2p2_st_{k-2}"))
                    e.tensor_tensor(out=i8st[k % 2][:], in0=wst[k % 2][:], in1=ps_bc[:],
                                    op=ALU.mult).then_inc(sem_v, 1)
                    c.inc("v")
                    c.mark(f"w2p2_q_{k}", "v")

                @_bsync
                def _(e, dr=dr, fc=fc):
                    k = c.get("w2p2")
                    e.wait_ge(sem_v, c.at(f"w2p2_q_{k}"))
                    e.dma_start(out=w2i8_hbm[fc * 128:(fc + 1) * 128, dr * 512:(dr + 1) * 512],
                                in_=i8st[k % 2][:]).then_inc(sem_y, 16)
                    c.inc("y", 16)
                    c.mark(f"w2p2_st_{k}", "y")
                    c.inc("w2p2")

            @_bvec
            def _(e, dr=dr):
                e.wait_ge(sem_y, c.at(f"w2p2_st_{c.get('w2p2')-1}"))
                e.fin(sem_v, 1)
                c.inc("v")
                c.mark(f"w2q_{dr}_done", "v")

        barrier()
        w2pctx.close()

        b2ctx = ExitStack()
        u8h = [sb(f"u8h{i}", [128, T], U8, "right", b2ctx) for i in range(2)]
        for fc in range(NFC):
            @_bscal
            def _(e, fc=fc):
                if fc >= 2:
                    e.wait_ge(sem_g, c.at(f"hq_{fc-2}_cast"))
                e.activation(out=u8h[fc % 2][:], in_=h_sb[fc][:], func=AF.Identity,
                             scale=sc[:, 2:3]).then_inc(sem_s, 1)
                c.inc("s")
                c.mark(f"hq_{fc}_u8", "s")

            @_bgps
            def _(e, fc=fc):
                e.wait_ge(sem_s, c.at(f"hq_{fc}_u8"))
                e.dma_start(out=h_sb[fc][:], in_=u8h[fc % 2][:]).then_inc(sem_g, 16)
                c.inc("g", 16)
                c.mark(f"hq_{fc}_cast", "g")

        @_bgps
        def _(e):
            e.wait_ge(sem_g, c.at(f"hq_{NFC-1}_cast"))

        barrier()

        w2jit = [sb(f"w2jit{i}", [128, 4096], BF16, "right", b2ctx) for i in range(2)]
        yT_sb = sb("yT_sb", [128, T], F32, "right", b2ctx)
        yst = [sb(f"yst{i}", [128, 128], F32, "right", b2ctx) for i in range(4)]

        for dc in range(NDC):
            @_bgps
            def _(e, dc=dc):
                if dc >= 2:
                    e.wait_ge(sem_t, c.at(f"f2_{dc-2}_mm"))
                src = w2i8_hbm.rearrange("(fc p) d -> p fc d", p=128)[:, :, dc * 128:(dc + 1) * 128]
                dst = w2jit[dc % 2][:].rearrange("p (fc d) -> p fc d", d=128)
                e.dma_start(out=dst, in_=src).then_inc(sem_g, 16)
                c.inc("g", 16)
                c.mark(f"f2_{dc}_w", "g")

            @_btens
            def _(e, dc=dc):
                e.wait_ge(sem_g, c.at(f"f2_{dc}_w"))
                if dc >= 1:
                    e.wait_ge(sem_s, c.at(f"f2_{dc-1}_y"))
                for fc in range(NFC):
                    for tq in range(4):
                        e.matmul(out=ps_f[tq][:],
                                 lhsT=w2jit[dc % 2][:, fc * 128:(fc + 1) * 128],
                                 rhs=h_sb[fc][:, tq * 512:(tq + 1) * 512],
                                 start=(fc == 0), stop=(fc == NFC - 1))
                e.fin(sem_t, 1)
                c.inc("t")
                c.mark(f"f2_{dc}_mm", "t")

            @_bscal
            def _(e, dc=dc):
                e.wait_ge(sem_t, c.at(f"f2_{dc}_mm"))
                if dc >= 1:
                    e.wait_ge(sem_t, c.at(f"f2_{dc-1}_trdone"))
                for tq in range(4):
                    e.activation(out=yT_sb[:, tq * 512:(tq + 1) * 512], in_=ps_f[tq][:],
                                 func=AF.Identity, bias=b2q[:, dc:dc + 1],
                                 scale=scale2[:, dc:dc + 1])
                e.fin(sem_s, 1)
                c.inc("s")
                c.mark(f"f2_{dc}_y", "s")

            for tg in range(4):
                @_btens
                def _(e, dc=dc, tg=tg):
                    k = c.get("f2tr")
                    e.wait_ge(sem_s, c.at(f"f2_{dc}_y"))
                    if k >= 1:
                        e.wait_ge(sem_v, c.at(f"f2ev_{k-1}"))
                    for b in range(4):
                        tc = tg * 4 + b
                        e.transpose(out=ps_tr[:, b * 128:(b + 1) * 128],
                                    in_=yT_sb[:, tc * 128:(tc + 1) * 128], identity=id128[:])
                    e.fin(sem_t, 1)
                    c.inc("t")
                    c.mark(f"f2tr_{k}", "t")

                @_bvec
                def _(e, dc=dc, tg=tg):
                    k = c.get("f2tr")
                    e.wait_ge(sem_t, c.at(f"f2tr_{k}"))
                    if k >= 1:
                        e.wait_ge(sem_g, c.at(f"f2acc_{k-1}"))
                    for b in range(4):
                        e.tensor_copy(out=yst[b][:], in_=ps_tr[:, b * 128:(b + 1) * 128])
                    e.fin(sem_v, 1)
                    c.inc("v")
                    c.mark(f"f2ev_{k}", "v")

                @_bgps
                def _(e, dc=dc, tg=tg):
                    k = c.get("f2tr")
                    e.wait_ge(sem_v, c.at(f"f2ev_{k}"))
                    for b in range(4):
                        tc = tg * 4 + b
                        e.dma_start(out=out[tc * 128:(tc + 1) * 128, dc * 128:(dc + 1) * 128],
                                    in_=yst[b][:], accum_op=ALU.add).then_inc(sem_g, 16)
                    c.inc("g", 64)
                    c.mark(f"f2acc_{k}", "g")
                    c.inc("f2tr")

            c.mark(f"f2_{dc}_trdone", "t")

        @_bsync
        def _(e):
            e.wait_ge(sem_g, c.at(f"f2acc_{c.get('f2tr')-1}"))

        barrier()
        b2ctx.close()
        pctxb.close()
        bctx.close()

    return nc


_NC_CACHE = {}


def kernel(L0_attn_x, L0_attn_W_qkv, L0_attn_b_qkv, L0_ffn_w1, L0_ffn_b1,
           L0_ffn_w2, L0_ffn_b2):
    x = np.ascontiguousarray(np.asarray(L0_attn_x, dtype=np.float32))
    wqkv = np.ascontiguousarray(np.asarray(L0_attn_W_qkv, dtype=np.float32))
    bqkv = np.ascontiguousarray(np.asarray(L0_attn_b_qkv, dtype=np.float32))
    w1t = np.ascontiguousarray(np.asarray(L0_ffn_w1, dtype=np.float32).T)
    b1 = np.ascontiguousarray(np.asarray(L0_ffn_b1, dtype=np.float32))
    w2t = np.ascontiguousarray(np.asarray(L0_ffn_w2, dtype=np.float32).T)
    b2 = np.ascontiguousarray(np.asarray(L0_ffn_b2, dtype=np.float32))

    if "nc" not in _NC_CACHE:
        _NC_CACHE["nc"] = build_nc()
    nc = _NC_CACHE["nc"]

    in_maps = [
        {"x": x[i], "wqkv": wqkv, "bqkv": bqkv, "w1t": w1t, "b1": b1,
         "w2t": w2t, "b2": b2}
        for i in range(8)
    ]
    res = run_bass_kernel_spmd(nc, in_maps, core_ids=list(range(8)))
    outs = [np.asarray(res.results[i]["out"]) for i in range(8)]
    return np.stack(outs, axis=0).astype(np.float32)


# version 09
# speedup vs baseline: 2.4580x; 2.4580x over previous
import sys
sys.path.insert(0, "/opt/trn_rl_repo")
from contextlib import ExitStack

import numpy as np

import concourse.bass as bass
import concourse.mybir as mybir
from concourse.bass_utils import run_bass_kernel_spmd
from concourse.replica_groups import maybe_share_collective_output_space

F32 = mybir.dt.float32
BF16 = mybir.dt.bfloat16
U8 = mybir.dt.uint8
I8 = mybir.dt.int8
I32 = mybir.dt.int32
AF = mybir.ActivationFunctionType
ALU = mybir.AluOpType
AX = mybir.AxisListType

T = 2048
D = 1024
H = 16
F = 4096
NTC = T // 128
NDC = D // 128
NFC = F // 128
CEXP = 60.0
LN_EPS = 1e-5
Q_EPS = 1e-8
BATCHES = [3, 3, 3, 3, 3, 1]
NB = len(BATCHES)
JB0 = [sum(BATCHES[:k]) for k in range(NB)]
NS = 14


class Counts:
    def __init__(self):
        self.n = {}
        self.marks = {}

    def inc(self, key, amt=1):
        self.n[key] = self.n.get(key, 0) + amt
        return self.n[key]

    def get(self, key):
        return self.n.get(key, 0)

    def mark(self, name, key):
        self.marks[name] = self.n.get(key, 0)

    def set_mark(self, name, val):
        self.marks[name] = val

    def at(self, name):
        return self.marks[name]


def build_nc():
    nc = bass.Bass("TRN2", target_bir_lowering=False, detect_race_conditions=False)

    x_in = nc.dram_tensor("x", [T, D], F32, kind="ExternalInput").ap()
    wqkv = nc.dram_tensor("wqkv", [D, 3 * D], F32, kind="ExternalInput").ap()
    bqkv = nc.dram_tensor("bqkv", [3 * D], F32, kind="ExternalInput").ap()
    w1n = nc.dram_tensor("w1n", [F, D], F32, kind="ExternalInput").ap()
    w1t = nc.dram_tensor("w1t", [D, F], F32, kind="ExternalInput").ap()
    b1 = nc.dram_tensor("b1", [F], F32, kind="ExternalInput").ap()
    w2n = nc.dram_tensor("w2n", [D, F], F32, kind="ExternalInput").ap()
    w2t = nc.dram_tensor("w2t", [F, D], F32, kind="ExternalInput").ap()
    b2 = nc.dram_tensor("b2", [D], F32, kind="ExternalInput").ap()
    out = nc.dram_tensor("out", [T, D], F32, kind="ExternalOutput").ap()

    mxrow_hbm = nc.dram_tensor("mxrow_hbm", [1, F], F32, kind="Internal").ap()
    mxrow2_hbm = nc.dram_tensor("mxrow2_hbm", [1, D], F32, kind="Internal").ap()

    rg = [list(range(8))]
    sp1 = maybe_share_collective_output_space("AllGather", rg)
    cc_in1 = nc.dram_tensor("cc_in1", [1], F32, kind="Internal")
    cc_out1 = nc.dram_tensor("cc_out1", [8], F32, kind="Internal", addr_space=sp1)
    cc_in2 = nc.dram_tensor("cc_in2", [1], F32, kind="Internal")
    cc_out2 = nc.dram_tensor("cc_out2", [8], F32, kind="Internal", addr_space=sp1)

    id128_t = nc.inline_tensor(np.eye(128, dtype=np.float32), "id128")
    id65_t = nc.inline_tensor(np.eye(65, dtype=np.float32), "id65")

    c = Counts()
    ctx = ExitStack()
    with ctx:
        def sb(name, shape, dtype, side="left", stack=None):
            return (stack or ctx).enter_context(
                nc.sbuf_tensor(name, shape, dtype, side=side))

        id128L = sb("id128L", [128, 128], F32, "left")
        id128R = sb("id128R", [128, 128], F32, "right")
        id65R = sb("id65R", [65, 65], F32, "right")
        ones1f = sb("ones1f", [1, 128], F32)
        ones1b = sb("ones1b", [1, 128], BF16)
        cneg = sb("cneg", [128, 1], F32)
        chalf = sb("chalf", [128, 1], F32)
        bqk_sb = sb("bqk_sb", [128, 16], F32)
        bv_row = sb("bv_row", [1, D], BF16, "right")
        sc = sb("sc", [128, 8], F32)
        mx1 = sb("mx1", [128, 32], F32)
        scale1 = sb("scale1", [128, 32], F32)
        b1_sb = sb("b1_sb", [128, 32], F32)
        b1q = sb("b1q", [128, 32], F32)
        bi32 = sb("bi32", [128, 32], I32)
        bif = sb("bif", [128, 32], F32)
        mx2 = sb("mx2", [128, 8], F32)
        mx2t = sb("mx2t", [128, 4], F32)
        scale2 = sb("scale2", [128, 8], F32)
        b2_sb = sb("b2_sb", [128, 8], F32)
        b2q = sb("b2q", [128, 8], F32)
        tiny = sb("tiny", [128, 8], F32)
        scr32 = sb("scr32", [128, 32], F32)
        g_row = sb("g_row", [1, 8], F32)
        g_one = sb("g_one", [1, 4], F32)
        st32 = sb("st32", [32, 128], F32)
        bqk_nat = sb("bqk_nat", [16, 128], F32)
        b1_nat = sb("b1_nat", [32, 128], F32)
        b2_nat = sb("b2_nat", [8, 128], F32)
        runmax = sb("runmax", [128, 1], F32)
        runmax2 = sb("runmax2", [128, 1], F32)
        lnstat = sb("lnstat", [128, 192], F32)
        lnagg = sb("lnagg", [128, 32], F32)
        lnrstd = sb("lnrstd", [128, 16], F32)
        lnbias = sb("lnbias", [128, 16], F32)

        sem_y = ctx.enter_context(nc.semaphore("sem_y"))
        sem_g = ctx.enter_context(nc.semaphore("sem_g"))
        sem_t = ctx.enter_context(nc.semaphore("sem_t"))
        sem_s = ctx.enter_context(nc.semaphore("sem_s"))
        sem_v = ctx.enter_context(nc.semaphore("sem_v"))
        sem_b = ctx.enter_context(nc.semaphore("sem_b"))
        sem_c = ctx.enter_context(nc.semaphore("sem_c"))
        sem_a = ctx.enter_context(nc.semaphore("sem_a"))
        sem_r = ctx.enter_context(nc.semaphore("sem_r"))
        sem_q = ctx.enter_context(nc.semaphore("sem_q"))

        block = ctx.enter_context(nc.Block())

        class EP:
            def __init__(self, e):
                object.__setattr__(self, "_e", e)
                object.__setattr__(self, "_last", None)

            def __getattr__(self, name):
                attr = getattr(self._e, name)
                if callable(attr):
                    def call(*a, **k):
                        r = attr(*a, **k)
                        if hasattr(r, "then_inc"):
                            object.__setattr__(self, "_last", r)
                        return r
                    return call
                return attr

            def fin(self, sem, n=1):
                return self._last.then_inc(sem, n)

            def selfsync(self, sem, key):
                self._last.then_inc(sem, 1)
                c.inc(key)
                self._e.wait_ge(sem, c.get(key))

        def _wrap(dec):
            def w(f):
                dec(lambda e: f(EP(e)))
            return w

        _bsync = _wrap(block.sync)
        _bgps = _wrap(block.gpsimd)
        _btens = _wrap(block.tensor)
        _bscal = _wrap(block.scalar)
        _bvec = _wrap(block.vector)

        nbar = [0]

        def barrier():
            nbar[0] += 1
            tgt = 5 * nbar[0]
            for dec in (_bsync, _bgps, _btens, _bscal, _bvec):
                def _f(e, tgt=tgt):
                    e.drain().then_inc(sem_b, 1)
                    e.wait_ge(sem_b, tgt)
                dec(_f)

        def emit_ln_stats(pref, wait_xld):
            @_bvec
            def _(e):
                for tc in range(NTC):
                    if wait_xld:
                        e.wait_ge(sem_y, c.at(f"xld_{tc}"))
                    e.bn_stats(out=lnstat[:, tc * 12: tc * 12 + 6], in_=X[tc][:, 0:512])
                    e.bn_stats(out=lnstat[:, tc * 12 + 6: tc * 12 + 12], in_=X[tc][:, 512:1024])
                e.selfsync(sem_v, "v")
                for tc in range(NTC):
                    e.bn_aggr(out=lnagg[:, tc * 2: tc * 2 + 2], in_=lnstat[:, tc * 12: tc * 12 + 12])
                e.selfsync(sem_v, "v")
                e.tensor_scalar_add(out=lnrstd[:], in0=lnagg[:, 1::2], scalar1=LN_EPS)
                e.fin(sem_v, 1)
                c.inc("v")
                c.mark(pref + "_var", "v")

            @_bscal
            def _(e):
                e.wait_ge(sem_v, c.at(pref + "_var"))
                e.activation(out=lnrstd[:], in_=lnrstd[:], func=AF.Ln)
                e.selfsync(sem_s, "s")
                e.activation(out=lnrstd[:], in_=lnrstd[:], func=AF.Exp,
                             scale=chalf[:, 0:1]).then_inc(sem_s, 1)
                c.inc("s")
                c.mark(pref + "_rstd", "s")

            @_bvec
            def _(e):
                e.wait_ge(sem_s, c.at(pref + "_rstd"))
                e.scalar_tensor_tensor(out=lnbias[:], in0=lnagg[:, 0::2], scalar=-1.0,
                                       in1=lnrstd[:], op0=ALU.mult, op1=ALU.mult).then_inc(sem_v, 1)
                c.inc("v")
                c.mark(pref + "_bias", "v")

        actx = ExitStack()
        X = [sb(f"X{i}", [128, D], F32, "left", actx) for i in range(NTC)]

        atnctx = ExitStack()
        V_sb = [sb(f"V{i}", [128, H * 65], BF16, "left", atnctx) for i in range(NTC)]
        QK = [[sb(f"QK{s}_{q}", [128, T], BF16, "right" if q == 0 else "left", atnctx)
               for q in range(2)] for s in range(2)]
        PT = [sb(f"PT{i}", [128, 1536], BF16, "right", atnctx) for i in range(2)]
        attS = sb("attS", [65, 512], F32, "left", atnctx)
        stg = [sb(f"stg{i}", [128, 64], F32, "left", atnctx) for i in range(4)]
        piece = sb("piece", [128, D], F32, "left", atnctx)
        lnst = [sb(f"lnst{i}", [128, D], F32, "right", atnctx) for i in range(2)]
        ln1T = [sb(f"ln1T{i}", [128, T], BF16, "left", atnctx) for i in range(NDC)]
        wvctx = ExitStack()
        wv_sb = [sb(f"wv{i}", [128, D], BF16, "right", wvctx) for i in range(NDC)]

        pctx = ExitStack()
        ps_ping = pctx.enter_context(nc.psum_tensor("ps_ping", [128, 1536], F32))
        ps_pong = pctx.enter_context(nc.psum_tensor("ps_pong", [128, 1536], F32))
        ps_att = pctx.enter_context(nc.psum_tensor("ps_att", [128, 512], F32))
        ps_misc = pctx.enter_context(nc.psum_tensor("ps_misc", [128, 512], F32))

        @_bsync
        def _(e):
            e.dma_start(out=id128L[:], in_=id128_t.ap()).then_inc(sem_y, 16)
            e.dma_start(out=id128R[:], in_=id128_t.ap()).then_inc(sem_y, 16)
            e.dma_start(out=id65R[:], in_=id65_t.ap()).then_inc(sem_y, 16)
            e.dma_start(out=bqk_nat[:], in_=bqkv[0:2048].rearrange("(j p) -> j p", p=128)).then_inc(sem_y, 16)
            e.dma_start(out=b1_nat[:], in_=b1.rearrange("(j p) -> j p", p=128)).then_inc(sem_y, 16)
            e.dma_start(out=b2_nat[:], in_=b2.rearrange("(j p) -> j p", p=128)).then_inc(sem_y, 16)
            c.inc("y", 16 * 6)
            c.mark("p0_small", "y")
            for i in range(NTC):
                e.dma_start(out=X[i][:], in_=x_in[i * 128:(i + 1) * 128, :]).then_inc(sem_y, 16)
                c.inc("y", 16)
                c.mark(f"xld_{i}", "y")
            e.wait_ge(sem_y, c.at("p0_small"))

        @_bgps
        def _(e):
            for dc in range(NDC):
                e.dma_start(out=wv_sb[dc][:], in_=wqkv[dc * 128:(dc + 1) * 128, 2048:3072]).then_inc(sem_g, 16)
            e.dma_start(out=bv_row[:], in_=bqkv[None, 2048:3072]).then_inc(sem_g, 16)
            c.inc("g", 16 * 9)
            c.mark("p0_casts", "g")
            e.wait_ge(sem_g, c.at("p0_casts"))

        @_bvec
        def _(e):
            e.memset(ones1f[:], 1.0)
            e.memset(ones1b[:], 1.0)
            e.memset(cneg[:], -CEXP)
            e.memset(chalf[:], -0.5)
            e.memset(runmax[:], 0.0)
            e.memset(runmax2[:], 0.0)
            for i in range(NTC):
                e.memset(V_sb[i][:].rearrange("p (h o) -> p h o", o=65)[:, :, 64:65], 1.0)
            c.inc("v")
            e.fin(sem_v, 1)

        barrier()

        @_btens
        def _(e):
            e.transpose(out=ps_misc[:, 0:16], in_=bqk_nat[:], identity=id128L[0:16, 0:16])
            e.transpose(out=ps_misc[:, 16:48], in_=b1_nat[:], identity=id128L[0:32, 0:32])
            e.transpose(out=ps_misc[:, 48:56], in_=b2_nat[:], identity=id128L[0:8, 0:8])
            e.fin(sem_t, 1)
            c.inc("t")
            c.mark("bias_tr", "t")

        @_bvec
        def _(e):
            e.wait_ge(sem_t, c.at("bias_tr"))
            e.tensor_copy(out=bqk_sb[:], in_=ps_misc[:, 0:16])
            e.tensor_copy(out=b1_sb[:], in_=ps_misc[:, 16:48])
            e.tensor_copy(out=b2_sb[:], in_=ps_misc[:, 48:56])
            e.fin(sem_v, 1)
            c.inc("v")
            c.mark("bias_ok", "v")

        emit_ln_stats("ln1", wait_xld=True)

        s_base = c.get("s")
        for tc in range(NTC):
            c.set_mark(f"ln1_{tc}_apply", s_base + tc + 1)
        c.inc("s", NTC)

        def emit_v_blocks(tcv):
            for g in range(2):
                k = 2 * tcv + g

                @_btens
                def _(e, tcv=tcv, g=g, k=k):
                    buf = ps_ping if tcv % 2 == 0 else ps_pong
                    bank = buf[:, g * 512:(g + 1) * 512]
                    if g == 0:
                        e.wait_ge(sem_v, c.at(f"ln1_{tcv}_evac"))
                    for dc in range(NDC):
                        e.matmul(out=bank, lhsT=ln1T[dc][:, tcv * 128:(tcv + 1) * 128],
                                 rhs=wv_sb[dc][:, g * 512:(g + 1) * 512],
                                 start=(dc == 0), stop=False)
                    e.matmul(out=bank, lhsT=ones1b[:], rhs=bv_row[:, g * 512:(g + 1) * 512],
                             start=False, stop=True)
                    e.fin(sem_t, 1)
                    c.inc("t")
                    c.mark(f"vc_{k}_mm", "t")

                @_bvec
                def _(e, tcv=tcv, g=g, k=k):
                    buf = ps_ping if tcv % 2 == 0 else ps_pong
                    bank = buf[:, g * 512:(g + 1) * 512]
                    e.wait_ge(sem_t, c.at(f"vc_{k}_mm"))
                    src = bank.rearrange("p (h o) -> p h o", o=64)
                    dst = V_sb[tcv][:].rearrange("p (h o) -> p h o", o=65)[:, g * 8:(g + 1) * 8, 0:64]
                    e.tensor_copy(out=dst, in_=src).then_inc(sem_v, 1)
                    c.inc("v")
                    c.mark(f"vc_{k}_evac", "v")

        for tc in range(NTC):
            @_btens
            def _(e, tc=tc):
                e.wait_ge(sem_s, c.at(f"ln1_{tc}_apply"))
                if tc >= 2:
                    e.wait_ge(sem_v, c.at(f"vc_{2*(tc-2)+1}_evac"))
                tgt = ps_ping if tc % 2 == 0 else ps_pong
                for dc in range(NDC):
                    e.transpose(out=tgt[:, dc * 128:(dc + 1) * 128],
                                in_=lnst[tc % 2][:, dc * 128:(dc + 1) * 128], identity=id128L[:])
                e.fin(sem_t, 1)
                c.inc("t")
                c.mark(f"ln1_{tc}_tr", "t")

            @_bvec
            def _(e, tc=tc):
                e.wait_ge(sem_t, c.at(f"ln1_{tc}_tr"))
                tgt = ps_ping if tc % 2 == 0 else ps_pong
                for dc in range(NDC):
                    e.tensor_copy(out=ln1T[dc][:, tc * 128:(tc + 1) * 128],
                                  in_=tgt[:, dc * 128:(dc + 1) * 128])
                e.fin(sem_v, 1)
                c.inc("v")
                c.mark(f"ln1_{tc}_evac", "v")

            if tc >= 1:
                emit_v_blocks(tc - 1)

        emit_v_blocks(NTC - 1)

        @_bscal
        def _(e):
            e.wait_ge(sem_v, c.at("ln1_bias"))
            for tc in range(NTC):
                if tc >= 2:
                    e.wait_ge(sem_t, c.at(f"ln1_{tc-2}_tr"))
                e.activation(out=lnst[tc % 2][:], in_=X[tc][:], func=AF.Identity,
                             bias=lnbias[:, tc:tc + 1], scale=lnrstd[:, tc:tc + 1]).then_inc(sem_s, 1)

        barrier()
        wvctx.close()
        jitw = [sb(f"jitw{i}", [128, 2048], BF16, "right", atnctx) for i in range(2)]
        wpc = [sb(f"wpc{i}", [128, 1024], F32, "right", atnctx) for i in range(2)]

        gb = [0]
        gu = [0]
        njit = [0]
        s_att_base = c.get("s")
        for n in range(NTC * 4 * NB):
            c.set_mark(f"exp_g{n}", s_att_base + n + 1)
        c.inc("s", NTC * 4 * NB)
        units = []
        pend_epi = [None]
        un_meta = {}

        def emit_jit_w(hp):
            @_bgps
            def _(e):
                if hp >= 2:
                    e.wait_ge(sem_t, c.at(f"qk_{hp-2}_mmdone"))
                for qk in range(2):
                    fs = qk * 1024 + hp * 128
                    src = wqkv.rearrange("(dc p) f -> p dc f", p=128)[:, :, fs:fs + 128]
                    dst = jitw[hp % 2][:].rearrange("p (qk dc f) -> p qk dc f", qk=2, f=128)[:, qk]
                    e.dma_start(out=dst, in_=src).then_inc(sem_g, 16)
                c.inc("g", 32)
                c.mark(f"qk_{hp}_w", "g")

        def emit_jit_ev(hp, gi):
            qk, tq = gi // 4, gi % 4

            @_bvec
            def _(e):
                e.wait_ge(sem_t, c.at(f"qk_{hp}_{gi}_mm"))
                e.tensor_scalar_add(out=QK[hp % 2][qk][:, tq * 512:(tq + 1) * 512],
                                    in0=ps_misc[:],
                                    scalar1=bqk_sb[:, qk * 8 + hp: qk * 8 + hp + 1]).then_inc(sem_q, 1)
            njit[0] += 1

        def emit_jit_half(e, hp, gi, half, un):
            qk, tq = gi // 4, gi % 4
            if half == 0:
                if gi == 0:
                    e.wait_ge(sem_g, c.at(f"qk_{hp}_w"))
                e.wait_ge(sem_q, njit[0])
                e.wait_ge(sem_r, un)
            for dc in range(4 * half, 4 * half + 4):
                e.matmul(out=ps_misc[:],
                         lhsT=jitw[hp % 2][:, qk * 1024 + dc * 128: qk * 1024 + (dc + 1) * 128],
                         rhs=ln1T[dc][:, tq * 512:(tq + 1) * 512],
                         start=(dc == 0), stop=(dc == NDC - 1))
            if half == 1:
                e.fin(sem_t, 1)
                c.inc("t")
                c.mark(f"qk_{hp}_{gi}_mm", "t")
                if gi == 7:
                    c.mark(f"qk_{hp}_mmdone", "t")

        def epi_norm(un):
            h, qq = un_meta[un]

            @_bvec
            def _(e):
                e.wait_ge(sem_t, c.at(f"tr_u{un}"))
                if un >= 1:
                    e.wait_ge(sem_y, c.at(f"fold_u{un-1}"))
                for b in range(4):
                    e.reciprocal(out=tiny[:, b:b + 1], in_=ps_misc[:, b * 65 + 64: b * 65 + 65])
                e.selfsync(sem_v, "v")
                for b in range(3):
                    e.tensor_scalar_mul(out=stg[b][:], in0=ps_misc[:, b * 65: b * 65 + 64],
                                        scalar1=tiny[:, b:b + 1])
                e.tensor_scalar_mul(out=stg[3][:], in0=ps_misc[:, 3 * 65: 3 * 65 + 64],
                                    scalar1=tiny[:, 3:4]).then_inc(sem_r, 1)

            @_bsync
            def _(e):
                e.wait_ge(sem_r, un + 1)
                if un >= 1:
                    e.wait_ge(sem_v, c.at(f"padd_u{un-1}"))
                r0 = qq * 32
                for b in range(4):
                    e.dma_start(out=piece[r0 + b * 8: r0 + (b + 1) * 8, :].rearrange("g (ci i) -> g ci i", i=64),
                                in_=stg[b][:]).then_inc(sem_y, 16)
                c.inc("y", 64)
                c.mark(f"fold_u{un}", "y")

        def epi_add(un):
            h, qq = un_meta[un]

            @_bvec
            def _(e):
                e.wait_ge(sem_y, c.at(f"fold_u{un}"))
                rows = slice(qq * 32, qq * 32 + 32)
                e.tensor_add(out=X[h][rows, :], in0=X[h][rows, :], in1=piece[rows, :]).then_inc(sem_v, 1)
                c.inc("v")
                c.mark(f"padd_u{un}", "v")

        def emit_attention_unit(h, qq, qset, jit):
            hh = h % 2
            un = gu[0]
            gu[0] += 1
            un_meta[un] = (h, qq)
            kb = [gb[0] + k for k in range(NB)]
            gb[0] += NB
            hp = h // 2
            first_of_hp = (hh == 0 and qq == 0)
            qsl = QK[qset][0][hh * 64: hh * 64 + 64, qq * 512:(qq + 1) * 512]
            prev = pend_epi[0]
            njit_now = njit[0]

            @_btens
            def _(e):
                if first_of_hp:
                    e.wait_ge(sem_q, 8 * (hp + 1))
                for k in range(NB):
                    n = kb[k]
                    buf = ps_ping if n % 2 == 0 else ps_pong
                    if n >= 2:
                        e.wait_ge(sem_s, c.at(f"exp_g{n-2}"))
                    for j in range(BATCHES[k]):
                        jb = JB0[k] + j
                        e.matmul(out=buf[:, j * 512:(j + 1) * 512],
                                 lhsT=QK[qset][1][hh * 64: hh * 64 + 64, jb * 128:(jb + 1) * 128],
                                 rhs=qsl, start=True, stop=True)
                    e.fin(sem_t, 1)
                    c.inc("t")
                    c.mark(f"qkt_g{n}", "t")
                    if k >= 1:
                        npv = kb[k - 1]
                        e.wait_ge(sem_s, c.at(f"exp_g{npv}"))
                        if k == 1:
                            e.wait_ge(sem_a, un)
                        for j in range(BATCHES[k - 1]):
                            jb = JB0[k - 1] + j
                            e.matmul(out=ps_att[0:65, :],
                                     lhsT=V_sb[jb][:, h * 65:(h + 1) * 65],
                                     rhs=PT[npv % 2][:, j * 512:(j + 1) * 512],
                                     start=(k == 1 and j == 0), stop=False)
                        e.fin(sem_t, 1)
                        c.inc("t")
                        c.mark(f"pv_g{npv}", "t")
                    if k == 1 and prev is not None:
                        e.wait_ge(sem_a, un)
                        e.wait_ge(sem_q, njit_now)
                        for b in range(4):
                            e.transpose(out=ps_misc[:, b * 65:(b + 1) * 65],
                                        in_=attS[:, b * 128:(b + 1) * 128], identity=id65R[:])
                        e.fin(sem_t, 1)
                        c.inc("t")
                        c.mark(f"tr_u{prev}", "t")
                    if jit is not None and k in (3, 4):
                        emit_jit_half(e, jit[0], jit[1], k - 3, un)
                n = kb[NB - 1]
                e.wait_ge(sem_s, c.at(f"exp_g{n}"))
                for j in range(BATCHES[NB - 1]):
                    jb = JB0[NB - 1] + j
                    e.matmul(out=ps_att[0:65, :],
                             lhsT=V_sb[jb][:, h * 65:(h + 1) * 65],
                             rhs=PT[n % 2][:, j * 512:(j + 1) * 512],
                             start=False, stop=(j == BATCHES[NB - 1] - 1))
                e.fin(sem_t, 1)
                c.inc("t")
                c.mark(f"pv_g{n}", "t")

            if prev is not None:
                epi_norm(prev)
            if jit is not None:
                emit_jit_ev(jit[0], jit[1])

            @_bvec
            def _(e):
                e.wait_ge(sem_t, c.at(f"pv_g{kb[-1]}"))
                e.tensor_copy(out=attS[:], in_=ps_att[0:65, :]).then_inc(sem_a, 1)

            if prev is not None:
                epi_add(prev)

            units.append((un, kb))
            pend_epi[0] = un

        def bg_w1(fc):
            def go():
                @_bsync
                def _(e):
                    k = c.get("wp1")
                    if k >= 2:
                        e.wait_ge(sem_v, c.at(f"wp1_{k-2}_red"))
                    e.dma_start(out=wpc[k % 2][:],
                                in_=w1n[fc * 128:(fc + 1) * 128, :]).then_inc(sem_y, 16)
                    c.inc("y", 16)
                    c.mark(f"wp1_{k}_ld", "y")

                @_bvec
                def _(e):
                    k = c.get("wp1")
                    e.wait_ge(sem_y, c.at(f"wp1_{k}_ld"))
                    e.reduce_max(out=mx1[:, fc:fc + 1], in_=wpc[k % 2][:], axis=AX.X,
                                 apply_absolute_value=True).then_inc(sem_v, 1)
                    c.inc("v")
                    c.mark(f"wp1_{k}_red", "v")
                    c.inc("wp1")
            return go

        def bg_w2(dc, half):
            def go():
                @_bsync
                def _(e):
                    k = c.get("wp1")
                    if k >= 2:
                        e.wait_ge(sem_v, c.at(f"wp1_{k-2}_red"))
                    e.dma_start(out=wpc[k % 2][:],
                                in_=w2n[dc * 128:(dc + 1) * 128,
                                        half * 1024:(half + 1) * 1024]).then_inc(sem_y, 16)
                    c.inc("y", 16)
                    c.mark(f"wp1_{k}_ld", "y")

                @_bvec
                def _(e):
                    k = c.get("wp1")
                    e.wait_ge(sem_y, c.at(f"wp1_{k}_ld"))
                    e.reduce_max(out=mx2t[:, half:half + 1], in_=wpc[k % 2][:], axis=AX.X,
                                 apply_absolute_value=True).then_inc(sem_v, 1)
                    c.inc("v")
                    c.mark(f"wp1_{k}_red", "v")
                    c.inc("wp1")
                    if half == 3:
                        e.wait_ge(sem_v, c.get("v"))
                        e.reduce_max(out=mx2[:, dc:dc + 1], in_=mx2t[:, 0:4],
                                     axis=AX.X).then_inc(sem_v, 1)
                        c.inc("v")
            return go

        bg = [bg_w1(fc) for fc in range(NFC)]
        for dc in range(NDC):
            for half in range(4):
                bg.append(bg_w2(dc, half))

        emit_jit_w(0)
        for gi in range(8):
            @_btens
            def _(e, gi=gi):
                if gi == 0:
                    e.wait_ge(sem_g, c.at("qk_0_w"))
                else:
                    e.wait_ge(sem_q, gi)
                qk, tq = gi // 4, gi % 4
                for dc in range(NDC):
                    e.matmul(out=ps_misc[:],
                             lhsT=jitw[0][:, qk * 1024 + dc * 128: qk * 1024 + (dc + 1) * 128],
                             rhs=ln1T[dc][:, tq * 512:(tq + 1) * 512],
                             start=(dc == 0), stop=(dc == NDC - 1))
                e.fin(sem_t, 1)
                c.inc("t")
                c.mark(f"qk_0_{gi}_mm", "t")
                if gi == 7:
                    c.mark("qk_0_mmdone", "t")
            emit_jit_ev(0, gi)

        bg_i = [0]
        for hp in range(8):
            if hp + 1 < 8:
                emit_jit_w(hp + 1)
            for hh in range(2):
                h = 2 * hp + hh
                for qq in range(4):
                    jit = (hp + 1, hh * 4 + qq) if hp + 1 < 8 else None
                    emit_attention_unit(h, qq, hp % 2, jit)
                    if bg_i[0] < len(bg):
                        bg[bg_i[0]]()
                        bg_i[0] += 1
        while bg_i[0] < len(bg):
            bg[bg_i[0]]()
            bg_i[0] += 1

        last_un = pend_epi[0]

        @_btens
        def _(e):
            e.wait_ge(sem_a, last_un + 1)
            e.wait_ge(sem_r, last_un)
            for b in range(4):
                e.transpose(out=ps_misc[:, b * 65:(b + 1) * 65],
                            in_=attS[:, b * 128:(b + 1) * 128], identity=id65R[:])
            e.fin(sem_t, 1)
            c.inc("t")
            c.mark(f"tr_u{last_un}", "t")

        epi_norm(last_un)
        epi_add(last_un)

        @_bscal
        def _(e):
            sdone = s_att_base
            for (un, kb) in units:
                for k in range(NB):
                    n = kb[k]
                    buf = ps_ping if n % 2 == 0 else ps_pong
                    e.wait_ge(sem_t, c.at(f"qkt_g{n}"))
                    if n >= 2:
                        e.wait_ge(sem_t, c.at(f"pv_g{n-2}"))
                    w = BATCHES[k] * 512
                    e.activation(out=PT[n % 2][:, 0:w], in_=buf[:, 0:w],
                                 func=AF.Exp, bias=cneg[:]).then_inc(sem_s, 1)
                    sdone += 1
            assert sdone == s_att_base + NTC * 4 * NB

        barrier()
        pctx.close()
        atnctx.close()

        ln2r = [sb(f"ln2r{i}", [128, D], F32, "left", actx) for i in range(NTC)]
        p4ctx = ExitStack()
        ps_a = p4ctx.enter_context(nc.psum_tensor("ps_a", [128, 1024], F32))

        emit_ln_stats("ln2", wait_xld=False)

        s_base2 = c.get("s")
        for tc in range(NTC):
            c.set_mark(f"ln2_{tc}_apply", s_base2 + tc + 1)
        c.inc("s", NTC)

        @_bvec
        def _(e):
            for tc in range(NTC):
                e.wait_ge(sem_s, c.at(f"ln2_{tc}_apply"))
                e.reduce_max(out=scr32[:, tc:tc + 1], in_=ln2r[tc][:], axis=AX.X).then_inc(sem_v, 1)
                c.inc("v")
            e.wait_ge(sem_v, c.get("v"))
            e.reduce_max(out=runmax[:], in_=scr32[:, 0:16], axis=AX.X).then_inc(sem_v, 1)
            c.inc("v")
            c.mark("ln2max", "v")

        @_bsync
        def _(e):
            for tc in range(NTC):
                e.dma_start(out=out[tc * 128:(tc + 1) * 128, :], in_=X[tc][:]).then_inc(sem_y, 16)
            c.inc("y", 16 * NTC)
            c.mark("spill_all", "y")

        @_bscal
        def _(e):
            e.wait_ge(sem_v, c.at("ln2_bias"))
            for tc in range(NTC):
                e.activation(out=ln2r[tc][:], in_=X[tc][:], func=AF.Relu,
                             bias=lnbias[:, tc:tc + 1], scale=lnrstd[:, tc:tc + 1]).then_inc(sem_s, 1)

        @_btens
        def _(e):
            e.wait_ge(sem_v, c.at("ln2max"))
            e.transpose(out=ps_a[0:1, 0:128], in_=runmax[:], identity=id128R[:]).then_inc(sem_t, 1)
            c.inc("t")
            c.mark("rmx_tr", "t")

        @_bvec
        def _(e):
            e.wait_ge(sem_t, c.at("rmx_tr"))
            e.reduce_max(out=g_one[0:1, 0:1], in_=ps_a[0:1, 0:128], axis=AX.X).then_inc(sem_v, 1)
            c.inc("v")
            c.mark("rmx_fold1", "v")

        def emit_allgather(tag, cin, cout):
            @_bgps
            def _(e):
                e.wait_ge(sem_v, c.at(f"rmx_fold{tag}"))
                e.dma_start(out=cin.ap(), in_=g_one[0:1, 0:1]).then_inc(sem_g, 16)
                c.inc("g", 16)
                e.wait_ge(sem_g, c.get("g"))
                e.collective_compute("AllGather", ALU.bypass, replica_groups=rg,
                                     ins=[cin.ap()], outs=[cout.ap()]).then_inc(sem_c, 1)
                c.inc("cc", 1)
                e.wait_ge(sem_c, c.get("cc"))
                e.dma_start(out=g_row[0:1, :], in_=cout.ap()[None, :]).then_inc(sem_g, 16)
                c.inc("g", 16)
                c.mark(f"ar_done{tag}", "g")

            @_bvec
            def _(e):
                e.wait_ge(sem_g, c.at(f"ar_done{tag}"))
                e.reduce_max(out=g_one[0:1, 0:1], in_=g_row[0:1, :], axis=AX.X).then_inc(sem_v, 1)
                c.inc("v")
                e.wait_ge(sem_v, c.get("v"))
                e.tensor_scalar_max(out=g_one[0:1, 1:2], in0=g_one[0:1, 0:1], scalar1=Q_EPS).then_inc(sem_v, 1)
                c.inc("v")
                e.wait_ge(sem_v, c.get("v"))
                e.reciprocal(out=g_row[0:1, 3:4], in_=g_one[0:1, 1:2]).then_inc(sem_v, 1)
                c.inc("v")
                e.wait_ge(sem_v, c.get("v"))
                e.tensor_scalar_mul(out=g_one[0:1, 2:3], in0=g_row[0:1, 3:4], scalar1=255.0)
                e.tensor_scalar_mul(out=g_one[0:1, 3:4], in0=g_one[0:1, 1:2], scalar1=1.0 / (255.0 * 127.0))
                e.fin(sem_v, 1)
                c.inc("v")
                c.mark(f"s_math{tag}", "v")

        emit_allgather("1", cc_in1, cc_out1)

        @_btens
        def _(e):
            e.wait_ge(sem_v, c.at("s_math1"))
            e.matmul(out=ps_a[:, 0:2], lhsT=ones1f[:], rhs=g_one[0:1, 2:4], start=True, stop=True)
            e.fin(sem_t, 1)
            c.inc("t")
            c.mark("s1_bc", "t")

        @_bvec
        def _(e):
            e.wait_ge(sem_t, c.at("s1_bc"))
            e.tensor_copy(out=sc[:, 0:2], in_=ps_a[:, 0:2]).then_inc(sem_v, 1)
            c.inc("v")
            c.mark("s1_ready", "v")

        @_bvec
        def _(e):
            e.wait_ge(sem_v, c.at(f"wp1_{c.get('wp1')-1}_red"))
            e.tensor_scalar_max(out=mx1[:], in0=mx1[:], scalar1=Q_EPS)
            e.tensor_scalar_max(out=mx2[:], in0=mx2[:], scalar1=Q_EPS).then_inc(sem_v, 1)
            c.inc("v")
            c.mark("mx_eps", "v")

        @_btens
        def _(e):
            e.wait_ge(sem_v, c.at("mx_eps"))
            e.transpose(out=ps_a[0:32, 0:128], in_=mx1[:], identity=id128R[:])
            e.transpose(out=ps_a[0:8, 128:256], in_=mx2[:], identity=id128R[:])
            e.fin(sem_t, 1)
            c.inc("t")
            c.mark("mxrow_tr", "t")

        @_bvec
        def _(e):
            e.wait_ge(sem_t, c.at("mxrow_tr"))
            e.reciprocal(out=st32[0:32, :], in_=ps_a[0:32, 0:128]).then_inc(sem_v, 1)
            c.inc("v")
            c.mark("mx1row_st", "v")

        @_bsync
        def _(e):
            e.wait_ge(sem_v, c.at("mx1row_st"))
            e.dma_start(out=mxrow_hbm.rearrange("o (fc p) -> o fc p", p=128),
                        in_=st32[0:32, :]).then_inc(sem_y, 16)
            c.inc("y", 16)
            c.mark("mx1row_dma", "y")

        @_bvec
        def _(e):
            e.wait_ge(sem_y, c.at("mx1row_dma"))
            e.reciprocal(out=st32[0:8, :], in_=ps_a[0:8, 128:256]).then_inc(sem_v, 1)
            c.inc("v")
            c.mark("mx2row_st", "v")

        @_bsync
        def _(e):
            e.wait_ge(sem_v, c.at("mx2row_st"))
            e.dma_start(out=mxrow2_hbm.rearrange("o (dc p) -> o dc p", p=128),
                        in_=st32[0:8, :]).then_inc(sem_y, 16)
            c.inc("y", 16)
            c.mark("mx2row_dma", "y")

        @_bvec
        def _(e):
            e.wait_ge(sem_v, c.at("s1_ready"))
            e.tensor_scalar_mul(out=scale1[:], in0=mx1[:], scalar1=sc[:, 1:2]).then_inc(sem_v, 1)
            c.inc("v")
            e.wait_ge(sem_v, c.get("v"))
            e.reciprocal(out=scr32[:], in_=scale1[:]).then_inc(sem_v, 1)
            c.inc("v")
            e.wait_ge(sem_v, c.get("v"))
            e.tensor_mul(out=b1q[:], in0=scr32[:], in1=b1_sb[:]).then_inc(sem_v, 1)
            c.inc("v")
            e.wait_ge(sem_v, c.get("v"))
            e.tensor_copy(out=bi32[:], in_=b1q[:]).then_inc(sem_v, 1)
            c.inc("v")
            c.mark("b1i32", "v")

        @_bgps
        def _(e):
            e.wait_ge(sem_v, c.at("b1i32"))
            e.dma_start(out=bif[:], in_=bi32[:]).then_inc(sem_g, 16)
            c.inc("g", 16)
            c.mark("b1if", "g")

        @_bvec
        def _(e):
            e.wait_ge(sem_g, c.at("b1if"))
            e.tensor_mul(out=b1q[:], in0=bif[:], in1=scale1[:]).then_inc(sem_v, 1)
            c.inc("v")
            c.mark("b1q_ok", "v")

        @_bsync
        def _(e):
            e.wait_ge(sem_y, c.at("spill_all"))

        barrier()
        p4ctx.close()

        pctxb = ExitStack()
        PS = [pctxb.enter_context(nc.psum_tensor(f"ps{i}", [128, 1024], F32))
              for i in range(4)]

        b1ctx = ExitStack()
        XiT = [sb(f"XiT{i}", [128, T], BF16, "right", b1ctx) for i in range(NDC)]
        p6ctx = ExitStack()
        u8st = [sb(f"u8st{i}", [128, D], U8, "right", p6ctx) for i in range(3)]

        s_base3 = c.get("s")
        for tc in range(NTC):
            c.set_mark(f"xq_{tc}_u8", s_base3 + tc + 1)
        c.inc("s", NTC)

        for tc in range(NTC):
            @_bgps
            def _(e, tc=tc):
                e.wait_ge(sem_s, c.at(f"xq_{tc}_u8"))
                e.dma_start(out=ln2r[tc][:], in_=u8st[tc % 3][:]).then_inc(sem_g, 16)
                c.inc("g", 16)
                c.mark(f"xq_{tc}_cast", "g")

            @_btens
            def _(e, tc=tc):
                e.wait_ge(sem_g, c.at(f"xq_{tc}_cast"))
                if tc >= 2:
                    e.wait_ge(sem_v, c.at(f"xq_{tc-2}_evac"))
                tgt = PS[tc % 2]
                for dc in range(NDC):
                    e.transpose(out=tgt[:, dc * 128:(dc + 1) * 128],
                                in_=ln2r[tc][:, dc * 128:(dc + 1) * 128], identity=id128R[:])
                e.fin(sem_t, 1)
                c.inc("t")
                c.mark(f"xq_{tc}_tr", "t")

            @_bvec
            def _(e, tc=tc):
                e.wait_ge(sem_t, c.at(f"xq_{tc}_tr"))
                tgt = PS[tc % 2]
                for dc in range(NDC):
                    e.tensor_copy(out=XiT[dc][:, tc * 128:(tc + 1) * 128],
                                  in_=tgt[:, dc * 128:(dc + 1) * 128])
                e.fin(sem_v, 1)
                c.inc("v")
                c.mark(f"xq_{tc}_evac", "v")

        @_bscal
        def _(e):
            for tc in range(NTC):
                if tc >= 3:
                    e.wait_ge(sem_g, c.at(f"xq_{tc-3}_cast"))
                e.activation(out=u8st[tc % 3][:], in_=ln2r[tc][:], func=AF.Identity,
                             scale=sc[:, 0:1]).then_inc(sem_s, 1)

        p6ctx.close()
        actx.close()

        bctx = ExitStack()
        h_sb = [sb(f"h{i}", [128, T], BF16, "left", bctx) for i in range(NFC)]
        w1f32 = [sb(f"w1f32{i}", [128, 1024], F32, "right", b1ctx) for i in range(2)]
        w1i8 = [sb(f"w1i8{i}", [128, 1024], I8, "right", b1ctx) for i in range(3)]
        w1jit = [sb(f"w1jit{i}", [128, 1024], BF16, "left", b1ctx) for i in range(3)]
        bcsl = [sb(f"bcsl{i}", [128, 128], F32, "right", b1ctx) for i in range(3)]

        s_base4 = c.get("s")
        for fc in range(NFC):
            c.set_mark(f"f1_{fc}_h", s_base4 + fc + 1)
        c.inc("s", NFC)

        def f1grp(fc, tq):
            return PS[2 * (fc % 2) + tq // 2][:, (tq % 2) * 512:(tq % 2) * 512 + 512]

        for fc in range(NFC):
            @_bsync
            def _(e, fc=fc):
                if fc == 0:
                    e.wait_ge(sem_y, c.at("mx1row_dma"))
                    e.wait_ge(sem_g, c.at(f"xq_{NTC-1}_cast"))
                if fc >= 2:
                    e.wait_ge(sem_v, c.at(f"f1_{fc-2}_q"))
                src = w1t.rearrange("(dc p) f -> p dc f", p=128)[:, :, fc * 128:(fc + 1) * 128]
                e.dma_start(out=w1f32[fc % 2][:].rearrange("p (dc f) -> p dc f", f=128),
                            in_=src).then_inc(sem_y, 16)
                e.dma_start(out=bcsl[fc % 3][:],
                            in_=mxrow_hbm[0:1, fc * 128:(fc + 1) * 128].broadcast_to([128, 128])).then_inc(sem_y, 16)
                c.inc("y", 32)
                c.mark(f"f1_{fc}_wld", "y")

            @_bvec
            def _(e, fc=fc):
                e.wait_ge(sem_y, c.at(f"f1_{fc}_wld"))
                if fc >= 3:
                    e.wait_ge(sem_g, c.at(f"f1_{fc-3}_wcast"))
                for dc in range(NDC):
                    e.scalar_tensor_tensor(out=w1i8[fc % 3][:, dc * 128:(dc + 1) * 128],
                                           in0=w1f32[fc % 2][:, dc * 128:(dc + 1) * 128],
                                           scalar=127.0, in1=bcsl[fc % 3][:],
                                           op0=ALU.mult, op1=ALU.mult)
                e.fin(sem_v, 1)
                c.inc("v")
                c.mark(f"f1_{fc}_q", "v")

            @_bgps
            def _(e, fc=fc):
                e.wait_ge(sem_v, c.at(f"f1_{fc}_q"))
                if fc >= 3:
                    e.wait_ge(sem_t, c.at(f"f1_{fc-3}_mm"))
                e.dma_start(out=w1jit[fc % 3][:], in_=w1i8[fc % 3][:]).then_inc(sem_g, 16)
                c.inc("g", 16)
                c.mark(f"f1_{fc}_wcast", "g")

            @_btens
            def _(e, fc=fc):
                e.wait_ge(sem_g, c.at(f"f1_{fc}_wcast"))
                if fc == 0:
                    e.wait_ge(sem_v, c.at(f"xq_{NTC-1}_evac"))
                if fc >= 2:
                    e.wait_ge(sem_s, c.at(f"f1_{fc-2}_h"))
                for dc in range(NDC):
                    for tq in range(4):
                        e.matmul(out=f1grp(fc, tq),
                                 lhsT=w1jit[fc % 3][:, dc * 128:(dc + 1) * 128],
                                 rhs=XiT[dc][:, tq * 512:(tq + 1) * 512],
                                 start=(dc == 0), stop=(dc == NDC - 1))
                e.fin(sem_t, 1)
                c.inc("t")
                c.mark(f"f1_{fc}_mm", "t")

            @_bvec
            def _(e, fc=fc):
                if fc >= 2:
                    e.wait_ge(sem_s, c.at(f"f1_{fc-2}_h"))
                    e.reduce_max(out=bif[:, fc - 2:fc - 1], in_=h_sb[fc - 2][:],
                                 axis=AX.X).then_inc(sem_v, 1)
                    c.inc("v")
                    c.mark(f"f1_{fc-2}_hmax", "v")

        @_bscal
        def _(e):
            for fc in range(NFC):
                e.wait_ge(sem_t, c.at(f"f1_{fc}_mm"))
                for tq in range(4):
                    e.activation(out=h_sb[fc][:, tq * 512:(tq + 1) * 512],
                                 in_=f1grp(fc, tq),
                                 func=AF.Relu, bias=b1q[:, fc:fc + 1], scale=scale1[:, fc:fc + 1])
                e.fin(sem_s, 1)

        @_bvec
        def _(e):
            for fc in range(NFC - 2, NFC):
                e.wait_ge(sem_s, c.at(f"f1_{fc}_h"))
                e.reduce_max(out=bif[:, fc:fc + 1], in_=h_sb[fc][:],
                             axis=AX.X).then_inc(sem_v, 1)
                c.inc("v")
            e.wait_ge(sem_v, c.get("v"))
            e.reduce_max(out=runmax2[:], in_=bif[:], axis=AX.X).then_inc(sem_v, 1)
            c.inc("v")
            c.mark("hmax_done", "v")

        barrier()
        b1ctx.close()

        @_btens
        def _(e):
            e.transpose(out=PS[0][0:1, 0:128], in_=runmax2[:], identity=id128R[:]).then_inc(sem_t, 1)
            c.inc("t")
            c.mark("rmx2_tr", "t")

        @_bvec
        def _(e):
            e.wait_ge(sem_t, c.at("rmx2_tr"))
            e.reduce_max(out=g_one[0:1, 0:1], in_=PS[0][0:1, 0:128], axis=AX.X).then_inc(sem_v, 1)
            c.inc("v")
            c.mark("rmx_fold2", "v")

        b2ctx = ExitStack()
        u8h = [sb(f"u8h{i}", [128, T], U8, "right", b2ctx) for i in range(4)]
        w2f32 = [sb(f"w2f32{i}", [128, 1024], F32, "right", b2ctx) for i in range(2)]
        w2i8 = [sb(f"w2i8{i}", [128, 1024], I8, "right", b2ctx) for i in range(3)]
        w2jit = [sb(f"w2jit{i}", [128, 4096], BF16, "right", b2ctx) for i in range(2)]
        yT_sb = [sb(f"yT{i}", [128, 1024], F32, "right", b2ctx) for i in range(2)]
        yev = [sb(f"yev{i}", [128, 1024], F32, "right", b2ctx) for i in range(2)]
        bcsl2 = [sb(f"bcsl2{i}", [128, 128], F32, "right", b2ctx) for i in range(2)]

        def w2_piece(dc, qtr):
            k = c.get("w2p")

            @_bsync
            def _(e, dc=dc, qtr=qtr, k=k):
                if k == 0:
                    e.wait_ge(sem_y, c.at("mx2row_dma"))
                if k >= 2:
                    e.wait_ge(sem_v, c.at(f"w2q_{k-2}"))
                src = w2t.rearrange("(fc p) d -> p fc d", p=128)[:, qtr * 8:(qtr + 1) * 8,
                                                                dc * 128:(dc + 1) * 128]
                e.dma_start(out=w2f32[k % 2][:].rearrange("p (fc d) -> p fc d", d=128),
                            in_=src).then_inc(sem_y, 16)
                if qtr == 0:
                    e.dma_start(out=bcsl2[dc % 2][:],
                                in_=mxrow2_hbm[0:1, dc * 128:(dc + 1) * 128].broadcast_to([128, 128])).then_inc(sem_y, 16)
                c.inc("y", 32 if qtr == 0 else 16)
                c.mark(f"w2ld_{k}", "y")

            @_bvec
            def _(e, dc=dc, qtr=qtr, k=k):
                e.wait_ge(sem_y, c.at(f"w2ld_{k}"))
                if k >= 3:
                    e.wait_ge(sem_g, c.at(f"w2cast_{k-3}"))
                for j in range(8):
                    e.scalar_tensor_tensor(out=w2i8[k % 3][:, j * 128:(j + 1) * 128],
                                           in0=w2f32[k % 2][:, j * 128:(j + 1) * 128],
                                           scalar=127.0, in1=bcsl2[dc % 2][:],
                                           op0=ALU.mult, op1=ALU.mult)
                e.fin(sem_v, 1)
                c.inc("v")
                c.mark(f"w2q_{k}", "v")

            @_bgps
            def _(e, dc=dc, qtr=qtr, k=k):
                e.wait_ge(sem_v, c.at(f"w2q_{k}"))
                if qtr == 0 and dc >= 2:
                    e.wait_ge(sem_s, c.at(f"f2_{2*(dc-2)+1}_y"))
                e.dma_start(out=w2jit[dc % 2][:, qtr * 1024:(qtr + 1) * 1024],
                            in_=w2i8[k % 3][:]).then_inc(sem_g, 16)
                c.inc("g", 16)
                c.mark(f"w2cast_{k}", "g")

            c.inc("w2p")

        for dc0 in range(2):
            for qtr in range(4):
                w2_piece(dc0, qtr)

        emit_allgather("2", cc_in2, cc_out2)

        @_btens
        def _(e):
            e.wait_ge(sem_v, c.at("s_math2"))
            e.matmul(out=PS[0][:, 0:2], lhsT=ones1f[:], rhs=g_one[0:1, 2:4], start=True, stop=True)
            e.fin(sem_t, 1)
            c.inc("t")
            c.mark("s2_bc", "t")

        @_bvec
        def _(e):
            e.wait_ge(sem_t, c.at("s2_bc"))
            e.tensor_copy(out=sc[:, 2:4], in_=PS[0][:, 0:2]).then_inc(sem_v, 1)
            c.inc("v")
            c.mark("s2_evac", "v")
            e.wait_ge(sem_v, c.at("s2_evac"))
            e.tensor_scalar_mul(out=scale2[:], in0=mx2[:], scalar1=sc[:, 3:4]).then_inc(sem_v, 1)
            c.inc("v")
            e.wait_ge(sem_v, c.get("v"))
            e.reciprocal(out=scr32[:, 0:8], in_=scale2[:]).then_inc(sem_v, 1)
            c.inc("v")
            e.wait_ge(sem_v, c.get("v"))
            e.tensor_mul(out=b2q[:], in0=scr32[:, 0:8], in1=b2_sb[:]).then_inc(sem_v, 1)
            c.inc("v")
            e.wait_ge(sem_v, c.get("v"))
            e.tensor_copy(out=bi32[:, 0:8], in_=b2q[:]).then_inc(sem_v, 1)
            c.inc("v")
            c.mark("b2i32", "v")

        @_bgps
        def _(e):
            e.wait_ge(sem_v, c.at("b2i32"))
            e.dma_start(out=bif[:, 0:8], in_=bi32[:, 0:8]).then_inc(sem_g, 16)
            c.inc("g", 16)
            c.mark("b2if", "g")

        @_bvec
        def _(e):
            e.wait_ge(sem_g, c.at("b2if"))
            e.tensor_mul(out=b2q[:], in0=bif[:, 0:8], in1=scale2[:]).then_inc(sem_v, 1)
            c.inc("v")
            c.mark("b2q_ok", "v")

        s_base5 = c.get("s")
        for fc in range(NS):
            c.set_mark(f"hq_{fc}_u8", s_base5 + fc + 1)
        c.inc("s", NS)
        v_base5 = c.get("v")
        for i, fc in enumerate(range(NS, NFC)):
            c.set_mark(f"hq_{fc}_u8v", v_base5 + i + 1)
        c.inc("v", NFC - NS)
        cast_order = []
        for i in range(NFC - NS):
            if i < NS:
                cast_order.append(i)
            cast_order.append(NS + i)
        for i in range(NFC - NS, NS):
            cast_order.append(i)
        g_base5 = c.get("g")
        for i, fc in enumerate(cast_order):
            c.set_mark(f"hq_{fc}_cast", g_base5 + 16 * (i + 1))
        c.inc("g", 16 * NFC)

        @_bscal
        def _(e):
            e.wait_ge(sem_v, c.at("s2_evac"))
            for fc in range(NS):
                if fc >= 2:
                    e.wait_ge(sem_g, c.at(f"hq_{fc-2}_cast"))
                e.activation(out=u8h[fc % 2][:], in_=h_sb[fc][:], func=AF.Identity,
                             scale=sc[:, 2:3]).then_inc(sem_s, 1)

        @_bvec
        def _(e):
            for fc in range(NS, NFC):
                if fc >= NS + 2:
                    e.wait_ge(sem_g, c.at(f"hq_{fc-2}_cast"))
                e.tensor_scalar_mul(out=u8h[2 + (fc - NS) % 2][:], in0=h_sb[fc][:],
                                    scalar1=sc[:, 2:3]).then_inc(sem_v, 1)

        for fc in cast_order:
            @_bgps
            def _(e, fc=fc):
                if fc < NS:
                    e.wait_ge(sem_s, c.at(f"hq_{fc}_u8"))
                    buf = u8h[fc % 2]
                else:
                    e.wait_ge(sem_v, c.at(f"hq_{fc}_u8v"))
                    buf = u8h[2 + (fc - NS) % 2]
                e.dma_start(out=h_sb[fc][:], in_=buf[:]).then_inc(sem_g, 16)

        @_bgps
        def _(e):
            e.wait_ge(sem_g, c.get("g"))

        barrier()

        s_base6 = c.get("s")
        for u in range(NDC * 2):
            c.set_mark(f"f2_{u}_y", s_base6 + u + 1)
        c.inc("s", NDC * 2)

        for dc in range(NDC):
            if dc + 2 < NDC:
                for qtr in range(4):
                    w2_piece(dc + 2, qtr)

            for half in range(2):
                u = 2 * dc + half

                @_btens
                def _(e, dc=dc, half=half, u=u):
                    if half == 0:
                        e.wait_ge(sem_g, c.at(f"w2cast_{4 * dc + 3}"))
                    if u == 0:
                        e.wait_ge(sem_v, c.at("s2_evac"))
                    if u >= 4:
                        e.wait_ge(sem_v, c.at(f"f2ev_{u-4}"))
                    grp = PS[u % 4]
                    for fc in range(NFC):
                        for tq in range(2):
                            e.matmul(out=grp[:, tq * 512:(tq + 1) * 512],
                                     lhsT=w2jit[dc % 2][:, fc * 128:(fc + 1) * 128],
                                     rhs=h_sb[fc][:, half * 1024 + tq * 512: half * 1024 + (tq + 1) * 512],
                                     start=(fc == 0), stop=(fc == NFC - 1))
                    e.fin(sem_t, 1)
                    c.inc("t")
                    c.mark(f"f2_{u}_mm", "t")

                @_btens
                def _(e, dc=dc, half=half, u=u):
                    e.wait_ge(sem_s, c.at(f"f2_{u}_y"))
                    grp = PS[u % 4]
                    for b in range(8):
                        e.transpose(out=grp[:, b * 128:(b + 1) * 128],
                                    in_=yT_sb[u % 2][:, b * 128:(b + 1) * 128], identity=id128L[:])
                    e.fin(sem_t, 1)
                    c.inc("t")
                    c.mark(f"f2tr_{u}", "t")

                @_bvec
                def _(e, dc=dc, half=half, u=u):
                    e.wait_ge(sem_t, c.at(f"f2tr_{u}"))
                    if u >= 2:
                        e.wait_ge(sem_g, c.at(f"f2acc_{u-2}"))
                    grp = PS[u % 4]
                    e.tensor_copy(out=yev[u % 2][:], in_=grp[:]).then_inc(sem_v, 1)
                    c.inc("v")
                    c.mark(f"f2ev_{u}", "v")

                @_bgps
                def _(e, dc=dc, half=half, u=u):
                    e.wait_ge(sem_v, c.at(f"f2ev_{u}"))
                    for b in range(8):
                        tc = half * 8 + b
                        e.dma_start(out=out[tc * 128:(tc + 1) * 128, dc * 128:(dc + 1) * 128],
                                    in_=yev[u % 2][:, b * 128:(b + 1) * 128], accum_op=ALU.add).then_inc(sem_g, 16)
                    c.inc("g", 16 * 8)
                    c.mark(f"f2acc_{u}", "g")

        @_bscal
        def _(e):
            e.wait_ge(sem_v, c.at("b2q_ok"))
            for u in range(NDC * 2):
                dc = u // 2
                e.wait_ge(sem_t, c.at(f"f2_{u}_mm"))
                if u >= 2:
                    e.wait_ge(sem_t, c.at(f"f2tr_{u-2}"))
                grp = PS[u % 4]
                for tq in range(2):
                    e.activation(out=yT_sb[u % 2][:, tq * 512:(tq + 1) * 512],
                                 in_=grp[:, tq * 512:(tq + 1) * 512],
                                 func=AF.Identity, bias=b2q[:, dc:dc + 1],
                                 scale=scale2[:, dc:dc + 1])
                e.fin(sem_s, 1)

        @_bsync
        def _(e):
            e.wait_ge(sem_g, c.at(f"f2acc_{NDC*2-1}"))

        barrier()
        b2ctx.close()
        pctxb.close()
        bctx.close()

    return nc


_NC_CACHE = {}


def kernel(L0_attn_x, L0_attn_W_qkv, L0_attn_b_qkv, L0_ffn_w1, L0_ffn_b1,
           L0_ffn_w2, L0_ffn_b2):
    x = np.ascontiguousarray(np.asarray(L0_attn_x, dtype=np.float32))
    wqkv = np.ascontiguousarray(np.asarray(L0_attn_W_qkv, dtype=np.float32))
    bqkv = np.ascontiguousarray(np.asarray(L0_attn_b_qkv, dtype=np.float32))
    w1np = np.ascontiguousarray(np.asarray(L0_ffn_w1, dtype=np.float32))
    w1tp = np.ascontiguousarray(w1np.T)
    b1 = np.ascontiguousarray(np.asarray(L0_ffn_b1, dtype=np.float32))
    w2np = np.ascontiguousarray(np.asarray(L0_ffn_w2, dtype=np.float32))
    w2tp = np.ascontiguousarray(w2np.T)
    b2 = np.ascontiguousarray(np.asarray(L0_ffn_b2, dtype=np.float32))

    if "nc" not in _NC_CACHE:
        _NC_CACHE["nc"] = build_nc()
    nc = _NC_CACHE["nc"]

    in_maps = [
        {"x": x[i], "wqkv": wqkv, "bqkv": bqkv, "w1n": w1np, "w1t": w1tp,
         "b1": b1, "w2n": w2np, "w2t": w2tp, "b2": b2}
        for i in range(8)
    ]
    res = run_bass_kernel_spmd(nc, in_maps, core_ids=list(range(8)))
    outs = [np.asarray(res.results[i]["out"]) for i in range(8)]
    return np.stack(outs, axis=0).astype(np.float32)
